# version 05
# baseline (speedup 1.0000x reference)
import math
import numpy as np
from contextlib import ExitStack

import concourse.bass as bass
import concourse.mybir as mybir
from concourse.bass_utils import run_bass_kernel_spmd

F32 = mybir.dt.float32
BF16 = mybir.dt.bfloat16
I32 = mybir.dt.int32
AF = mybir.ActivationFunctionType
ALU = mybir.AluOpType
AX = mybir.AxisListType.X

NCORE = 8
D_MODEL = 2048
D_INNER = 4096
NHEADS = 64
HD = 64
DSTATE = 128
CHUNK = 64
NEXP = 8
HPC = NHEADS // NCORE
KD = D_MODEL // 128
EPS = 1e-5
BIGNEG = -1e30
PI = math.pi

NIP = 1360
G1 = (0, 512)
G2 = (512, 848)
G3 = (848, 1360)

SEQ = 8192
DBG = ()


def _tile_k_major(w_t):
    C = w_t.shape[1]
    return np.ascontiguousarray(
        w_t.reshape(KD, 128, C).transpose(1, 0, 2).reshape(128, KD * C))


def host_prepare(inputs, seq):
    TOKC = seq // NCORE
    u = np.asarray(inputs["L0_attn_u"], np.float32).reshape(seq, D_MODEL)
    W_in = np.asarray(inputs["L0_attn_W_in"], np.float32)
    A_log = np.asarray(inputs["L0_attn_A_log"], np.float32)
    Dv = np.asarray(inputs["L0_attn_D"], np.float32)
    dt_bias = np.asarray(inputs["L0_attn_dt_bias"], np.float32)
    B_nw = np.asarray(inputs["L0_attn_B_norm_w"], np.float32)
    C_nw = np.asarray(inputs["L0_attn_C_norm_w"], np.float32)
    out_nw = np.asarray(inputs["L0_attn_out_norm_w"], np.float32)
    W_out = np.asarray(inputs["L0_attn_W_out"], np.float32)
    gw = np.asarray(inputs["L0_ffn_gate_weight"], np.float32)
    gu = np.asarray(inputs["L0_ffn_gate_up_proj"], np.float32)
    dp = np.asarray(inputs["L0_ffn_down_proj"], np.float32)

    tri = np.triu(np.ones((CHUNK, CHUNK), np.float32))
    triblk = np.zeros((128, 128), np.float32)
    triblk[:64, :64] = tri
    triblk[64:, 64:] = tri
    hi = np.tril(np.ones((CHUNK, CHUNK), np.float32), -1)
    triblk_hi = np.zeros((128, 128), np.float32)
    triblk_hi[:64, :64] = hi
    triblk_hi[64:, 64:] = hi
    triu128 = np.triu(np.ones((128, 128), np.float32))
    ind2neg = np.zeros((2, 128), np.float32)
    ind2neg[0, :64] = -1.0
    ind2neg[1, 64:] = -1.0
    delta8 = np.zeros((8, 512), np.float32)
    for h in range(8):
        delta8[h, h * 64:(h + 1) * 64] = 1.0
    bad = (np.arange(64)[:, None] > np.arange(64)[None, :]).astype(np.float32)
    tribias = np.zeros((128, 512), np.float32)
    for h in range(8):
        tribias[:64, h * 64:(h + 1) * 64] = bad * BIGNEG
        tribias[64:, h * 64:(h + 1) * 64] = bad * BIGNEG
    chunkcol = np.zeros((128, 2), np.float32)
    chunkcol[:64, 0] = 1.0
    chunkcol[64:, 1] = 1.0
    deltacol = np.zeros((8, 8 * 128), np.float32)
    for e in range(8):
        deltacol[e, e * 128:(e + 1) * 128] = 1.0
    ident = np.eye(128, dtype=np.float32)

    guT = np.ascontiguousarray(
        gu.transpose(0, 2, 1).reshape(NEXP, KD, 128, 64, 128)
        .transpose(0, 3, 2, 1, 4).reshape(NEXP, 64, 128, KD * 128))
    dpT = np.ascontiguousarray(
        dp.transpose(0, 2, 1).reshape(NEXP, 32, 128, 16, 128)
        .transpose(0, 3, 2, 1, 4).reshape(NEXP, 16, 128, 32 * 128))
    gwT = _tile_k_major(np.ascontiguousarray(gw.T))

    in_maps = []
    for core in range(NCORE):
        h0 = core * HPC
        rows = np.concatenate([
            W_in[D_INNER + h0 * HD:D_INNER + (h0 + HPC) * HD],
            W_in[2 * D_INNER:2 * D_INNER + DSTATE],
            W_in[2 * D_INNER + DSTATE:2 * D_INNER + 2 * DSTATE],
            W_in[2 * D_INNER + 2 * DSTATE + h0:
                 2 * D_INNER + 2 * DSTATE + h0 + HPC],
            W_in[2 * D_INNER + 2 * DSTATE + NHEADS + h0:
                 2 * D_INNER + 2 * DSTATE + NHEADS + h0 + HPC],
            W_in[2 * D_INNER + 2 * DSTATE + 2 * NHEADS:],
            W_in[h0 * HD:(h0 + HPC) * HD],
        ], axis=0)
        in_maps.append({
            "u": np.ascontiguousarray(u),
            "u_own": np.ascontiguousarray(u[core * TOKC:(core + 1) * TOKC]),
            "w_inT": _tile_k_major(np.ascontiguousarray(rows.T)),
            "w_outT": np.ascontiguousarray(
                np.ascontiguousarray(W_out[:, h0 * HD:(h0 + HPC) * HD].T)
                .reshape(4, 128, D_MODEL).transpose(1, 0, 2)
                .reshape(128, 4 * D_MODEL)),
            "gwT": gwT, "guT": guT, "dpT": dpT,
            "outnw": np.ascontiguousarray(
                out_nw[h0 * HD:(h0 + HPC) * HD].reshape(4, 128).T),
            "dtb_row": np.ascontiguousarray(dt_bias[None, h0:h0 + HPC]),
            "alog_row": np.ascontiguousarray(A_log[None, h0:h0 + HPC]),
            "bnw_row": np.ascontiguousarray(B_nw[None, :]),
            "cnw_row": np.ascontiguousarray(C_nw[None, :]),
            "d_hp": np.ascontiguousarray(
                np.repeat(Dv[h0:h0 + HPC], HD)[None, :]),
            "ident": ident, "triblk": triblk, "triblk_hi": triblk_hi,
            "triu128": triu128, "ind2neg": ind2neg, "delta8": delta8,
            "tribias": tribias, "chunkcol": chunkcol, "deltacol": deltacol,
        })
    return in_maps


def build_program(seq=SEQ, dbg=(), sim_cc_bypass=False, phases="full"):
    S = seq
    TOKC = S // NCORE
    NT = S // 128
    NB = S // 512
    NTO = TOKC // 128
    HALVES = [(i * 512, min(512, TOKC - i * 512))
              for i in range((TOKC + 511) // 512)]
    NHALF = len(HALVES)

    nc = bass.Bass("TRN2", target_bir_lowering=False,
                   detect_race_conditions=False)

    def din(name, shape, dtype=F32):
        return nc.dram_tensor(name, list(shape), dtype,
                              kind="ExternalInput").ap()

    u_d = din("u", (S, D_MODEL))
    uown_d = din("u_own", (TOKC, D_MODEL))
    w_inT_d = din("w_inT", (128, KD * NIP))
    w_outT_d = din("w_outT", (128, 4 * D_MODEL))
    gwT_d = din("gwT", (128, KD * 8))
    guT_d = din("guT", (NEXP, 64, 128, KD * 128))
    dpT_d = din("dpT", (NEXP, 16, 128, 32 * 128))
    outnw_d = din("outnw", (128, 4))
    dtb_d = din("dtb_row", (1, HPC))
    alog_d = din("alog_row", (1, HPC))
    bnw_d = din("bnw_row", (1, DSTATE))
    cnw_d = din("cnw_row", (1, DSTATE))
    dhp_d = din("d_hp", (1, 512))
    ident_d = din("ident", (128, 128))
    triblk_d = din("triblk", (128, 128))
    triblk_hi_d = din("triblk_hi", (128, 128))
    triu128_d = din("triu128", (128, 128))
    ind2neg_d = din("ind2neg", (2, 128))
    delta8_d = din("delta8", (8, 512))
    tribias_d = din("tribias", (128, 512))
    chunkcol_d = din("chunkcol", (128, 2))
    deltacol_d = din("deltacol", (8, 8 * 128))

    out_d = nc.dram_tensor("out", [TOKC, D_MODEL], F32,
                           kind="ExternalOutput").ap()

    xn_scr = nc.dram_tensor("xn_scr", [S, D_MODEL], BF16, kind="Internal").ap()
    xno_scr = nc.dram_tensor("xno_scr", [TOKC, D_MODEL], BF16,
                             kind="Internal").ap()
    flat_scr = nc.dram_tensor("flat_scr", [2, 8, 128], F32,
                              kind="Internal").ap()
    cc_in = nc.dram_tensor("cc_in", [S, D_MODEL], F32, kind="Internal").ap()
    cc_out = nc.dram_tensor("cc_out", [TOKC, D_MODEL], F32,
                            kind="Internal").ap()
    cc2_in = nc.dram_tensor("cc2_in", [S, 1], F32, kind="Internal").ap()
    cc2_out = nc.dram_tensor("cc2_out", [TOKC, 1], F32, kind="Internal").ap()
    gpart = nc.dram_tensor("gpart", [NEXP, D_MODEL, TOKC], F32,
                           kind="Internal").ap()
    gsum_d = nc.dram_tensor("gsum", [D_MODEL, TOKC], BF16, kind="Internal").ap()

    dbg_d = {}
    if "yg" in dbg:
        dbg_d["yg"] = nc.dram_tensor("dbg_yg", [S, 512], BF16,
                                     kind="ExternalOutput").ap()
    if "wexp" in dbg:
        dbg_d["wexp"] = nc.dram_tensor("dbg_wexp", [TOKC, 8], F32,
                                       kind="ExternalOutput").ap()
    if "y" in dbg:
        dbg_d["y"] = nc.dram_tensor("dbg_y", [S, 512], F32,
                                    kind="ExternalOutput").ap()
    if "drams" in dbg:
        dbg_d["xn"] = nc.dram_tensor("dbg_xn", [S, D_MODEL], BF16,
                                     kind="ExternalOutput").ap()
        dbg_d["ccin"] = nc.dram_tensor("dbg_ccin", [S, D_MODEL], F32,
                                       kind="ExternalOutput").ap()
        dbg_d["cc2in"] = nc.dram_tensor("dbg_cc2in", [S, 1], F32,
                                        kind="ExternalOutput").ap()
        dbg_d["gsum"] = nc.dram_tensor("dbg_gsum", [D_MODEL, TOKC], BF16,
                                       kind="ExternalOutput").ap()
        dbg_d["gp0"] = nc.dram_tensor("dbg_gp0", [D_MODEL, TOKC], F32,
                                      kind="ExternalOutput").ap()
        dbg_d["ccout"] = nc.dram_tensor("dbg_ccout", [TOKC, D_MODEL], F32,
                                        kind="ExternalOutput").ap()

    es = ExitStack()

    def sb(name, p, f, dtype, st=es):
        return st.enter_context(nc.sbuf_tensor(name, [p, f], dtype))

    class Sem:
        def __init__(self, name):
            self.h = es.enter_context(nc.semaphore(name))
            self.n = 0

        def inc(self, instr, k=1):
            instr.then_inc(self.h, k)
            self.n += k
            return self.n

    s_u = Sem("s_u")
    s_xt = Sem("s_xt")
    s_st = Sem("s_st")
    s_gw = Sem("s_gw")
    s_sh = Sem("s_sh")
    s_pe = Sem("s_pe")
    s_ve = Sem("s_ve")
    s_ac = Sem("s_ac")
    s_cc = Sem("s_cc")
    s_db = Sem("s_db")

    ident_bf = sb("ident_bf", 128, 128, BF16)
    ident_f = sb("ident_f", 128, 128, F32)
    ones1_bf = sb("ones1_bf", 1, 128, BF16)
    ones1_f = sb("ones1_f", 1, 128, F32)
    onescol_f = sb("onescol_f", 128, 1, F32)
    cst = sb("cst", 128, 4, F32)
    deltacol_bf = sb("deltacol_bf", 8, 8 * 128, BF16)
    gwT_s = sb("gwT_s", 128, KD * 8, BF16)
    vjunk = sb("vjunk", 128, 520, F32)
    junk = sb("junk", 128, D_MODEL, F32)

    PS = [es.enter_context(nc.psum_tensor("ps%d" % i, [128, 512], F32))
          for i in range(8)]
    B_YD, B_YO, B_MISC, B_LD, B_S1, B_S2, B_DEC, B_PRJ = range(8)
    bank_cons = [[] for _ in range(8)]

    def bank_touch(bank, sem):
        bank_cons[bank] = [(sem, sem.n)]

    def bank_wait(eng, bank):
        for sem, n in bank_cons[bank]:
            eng.wait_ge(sem.h, n)

    marks = {}

    def mark(key, sem):
        marks[key] = (sem, sem.n)

    def waitm(eng, key):
        if key in marks:
            sem, n = marks[key]
            eng.wait_ge(sem.h, n)

    blk = es.enter_context(nc.Block("prog"))

    attn_es = ExitStack()

    def sba(name, p, f, dtype):
        return sb(name, p, f, dtype, attn_es)

    triblk_f = sba("triblk_f", 128, 128, F32)
    triblk_hi_f = sba("triblk_hi_f", 128, 128, F32)
    triu128_f = sba("triu128_f", 128, 128, F32)
    ind2neg_f = sba("ind2neg_f", 2, 128, F32)
    delta8_f = sba("delta8_f", 8, 512, F32)
    tribias_f = sba("tribias_f", 128, 512, F32)
    chunkcol_f = sba("chunkcol_f", 128, 2, F32)
    dtb_b = sba("dtb_b", 128, HPC, F32)
    ealog_b = sba("ealog_b", 128, HPC, F32)
    bnw_b = sba("bnw_b", 128, DSTATE, F32)
    cnw_b = sba("cnw_b", 128, DSTATE, F32)
    d_b = sba("d_b", 128, 512, F32)
    row_tmp = sba("row_tmp", 1, 512, F32)
    w_inT = sba("w_inT_s", 128, KD * NIP, BF16)
    w_outT = sba("w_outT_s", 128, 4 * D_MODEL, BF16)
    outnw_s = sba("outnw_s", 128, 4, F32)
    u_t = [sba("u_t0", 128, D_MODEL, F32)]
    xnst = [sba("xnst0", 128, D_MODEL, BF16)]
    statv = sba("statv", 128, 8, F32)
    stata = sba("stata", 128, 8, F32)
    xnT = [sba("xnT%d" % i, 128, KD * 512, BF16) for i in range(2)]
    xh_bf = sba("xh_bf", 128, 4 * 512, BF16)
    xd1_bf = sba("xd1_bf", 128, 4 * 512, BF16)
    xd2r_bf = sba("xd2r_bf", 128, 4 * 512, BF16)
    xd2_bf = sba("xd2_bf", 128, 4 * 512, BF16)
    xds_bf = sba("xds_bf", 128, 2 * 512, BF16)
    zs_bf = sba("zs_bf", 128, 4 * 512, BF16)
    braw = sba("braw", 128, 4 * 128, F32)
    craw = sba("craw", 128, 4 * 128, F32)
    bn_bf = sba("bn_bf", 128, 4 * 128, BF16)
    cn_bf = sba("cn_bf", 128, 4 * 128, BF16)
    b2_bf = sba("b2_bf", 128, 4 * 128, BF16)
    bprev = [sba("bprev%d" % i, 1, 128, BF16) for i in range(2)]
    xprev = [sba("xprev%d" % i, 1, 512, BF16) for i in range(2)]
    b_fm = sba("b_fm", 128, 4 * 128 + 1, BF16)
    c_fm = sba("c_fm", 128, 4 * 128, BF16)
    theta_f = sba("theta_f", 128, 4 * 64, F32)
    ang_f = sba("ang_f", 128, 256, F32)
    redk = sba("redk", 128, 256, I32)
    trig = sba("trig", 128, 512, F32)
    off_row = sba("off_row", 1, 64, F32)
    dtpath = sba("dtpath", 128, 4 * 48, F32)
    a_f = sba("a_f", 128, 4 * 8, F32)
    acs_f = sba("acs_f", 128, 4 * 8, F32)
    nacst_f = sba("nacst_f", 8, 4 * 128, F32)
    nflat = [sba("nflat%d" % i, 2, 512, F32) for i in range(2)]
    dstate_f = sba("dstate_f", 128, 4 * 8, F32)
    eacs_f = sba("eacs_f", 128, 4 * 8, F32)
    eatot_row = sba("eatot_row", 1, 4 * 16, F32)
    decrow_bf = sba("decrow_bf", 1, 512, BF16)
    ld_bf = [sba("ld_bf%d" % i, 128, 512, BF16) for i in range(2)]
    mw_bf = sba("mw_bf", 128, 2 * 512, BF16)
    inter = [sba("inter%d" % i, 128, 512, F32) for i in range(2)]
    inter_bf = [sba("interb%d" % i, 128, 512, BF16) for i in range(2)]
    sctmp = sba("sctmp", 128, 512, F32)
    y_f = sba("y_f", 128, 512, F32)
    ytmp = sba("ytmp", 128, 512, F32)
    yg_bf = sba("yg_bf", 128, 4 * 512, BF16)
    y_fm = sba("y_fm", 128, 4 * 512, BF16)
    ssq = sba("ssq", 128, NT, F32)
    stage = [sba("stage0", 128, D_MODEL, F32)]

    @blk.sync
    def _(q):
        for ap_s, ap_d in [(triu128_f[:], triu128_d), (ident_f[:], ident_d),
                           (triblk_f[:], triblk_d),
                           (triblk_hi_f[:], triblk_hi_d),
                           (ind2neg_f[:], ind2neg_d), (delta8_f[:], delta8_d),
                           (chunkcol_f[:], chunkcol_d),
                           (tribias_f[:], tribias_d), (dtb_b[0:1, :], dtb_d),
                           (bnw_b[0:1, :], bnw_d), (cnw_b[0:1, :], cnw_d),
                           (ealog_b[0:1, :], alog_d),
                           (row_tmp[0:1, 0:512], dhp_d), (outnw_s[:], outnw_d)]:
            s_u.inc(q.dma_start(out=ap_s, in_=ap_d), 16)
    mark("setup_u", s_u)

    @blk.gpsimd
    def _(q):
        for ap_s, ap_d in [(ident_bf[:], ident_d),
                           (deltacol_bf[:], deltacol_d),
                           (w_inT[:], w_inT_d), (w_outT[:], w_outT_d),
                           (gwT_s[:], gwT_d)]:
            s_gw.inc(q.dma_start(out=ap_s, in_=ap_d), 16)
    mark("setup_gw", s_gw)

    @blk.vector
    def _(v):
        v.memset(cst[:, 0:1], -PI)
        v.memset(cst[:, 1:2], PI / 2)
        v.memset(cst[:, 2:3], 1.0)
        v.memset(cst[:, 3:4], EPS)
        v.memset(ones1_bf[:], 1.0)
        v.memset(ones1_f[:], 1.0)
        v.memset(onescol_f[:], 1.0)
        v.memset(off_row[:], 0.0)
        v.memset(bprev[0][:], 0.0)
        v.memset(bprev[1][:], 0.0)
        v.memset(xprev[0][:], 0.0)
        v.memset(xprev[1][:], 0.0)
        v.memset(inter[0][:], 0.0)
        v.memset(inter[1][:], 0.0)
        v.memset(inter_bf[0][:], 0.0)
        v.memset(inter_bf[1][:], 0.0)
        v.memset(b_fm[:, 0:1], 0.0)
        v.wait_ge(s_u.h, marks["setup_u"][1])
        s_ve.inc(v.engine_nop())
    mark("setup_ve", s_ve)

    @blk.tensor
    def _(t):
        t.wait_ge(s_ve.h, marks["setup_ve"][1])
        t.matmul(PS[B_MISC][:, 0:HPC], ones1_f[:], dtb_b[0:1, :],
                 start=True, stop=True)
        t.matmul(PS[B_MISC][:, 8:8 + HPC], ones1_f[:], ealog_b[0:1, :],
                 start=True, stop=True)
        t.matmul(PS[B_LD][:, 0:DSTATE], ones1_f[:], bnw_b[0:1, :],
                 start=True, stop=True)
        t.matmul(PS[B_LD][:, 128:128 + DSTATE], ones1_f[:], cnw_b[0:1, :],
                 start=True, stop=True)
        s_pe.inc(t.matmul(PS[B_YD][:, 0:512], ones1_f[:], row_tmp[0:1, 0:512],
                          start=True, stop=True))
    mark("setup_pe", s_pe)

    @blk.vector
    def _(v):
        v.wait_ge(s_pe.h, marks["setup_pe"][1])
        v.tensor_copy(dtb_b[:], PS[B_MISC][:, 0:HPC])
        v.tensor_copy(ealog_b[:], PS[B_MISC][:, 8:8 + HPC])
        v.tensor_copy(bnw_b[:], PS[B_LD][:, 0:DSTATE])
        v.tensor_copy(cnw_b[:], PS[B_LD][:, 128:128 + DSTATE])
        s_ve.inc(v.tensor_copy(d_b[:], PS[B_YD][:, 0:512]))
    mark("setup_bc", s_ve)
    for _b in (B_MISC, B_LD, B_YD):
        bank_touch(_b, s_ve)

    @blk.scalar
    def _(sc):
        sc.wait_ge(s_ve.h, marks["setup_bc"][1])
        s_ac.inc(sc.activation(ealog_b[:], ealog_b[:], AF.Exp))
    mark("setup_ac", s_ac)

    @blk.vector
    def _(v):
        v.wait_ge(s_gw.h, marks["setup_gw"][1])
        v.wait_ge(s_ac.h, marks["setup_ac"][1])
        for q in range(4):
            i = v.tensor_scalar_mul(w_outT[:, q * D_MODEL:(q + 1) * D_MODEL],
                                    w_outT[:, q * D_MODEL:(q + 1) * D_MODEL],
                                    outnw_s[:, q:q + 1])
        s_ve.inc(i)
    mark("setup_wout", s_ve)

    def emit_ln(b):
        for j in range(4):
            tl = b * 4 + j

            @blk.sync
            def _(q, tl=tl):
                waitm(q, ("lnve", tl - 1))
                s_u.inc(q.dma_start(out=u_t[0][:],
                                    in_=u_d[tl * 128:(tl + 1) * 128, :]), 16)
                mark(("lnu", tl), s_u)

            @blk.scalar
            def _(sc, tl=tl):
                waitm(sc, ("lnu", tl))
                waitm(sc, ("lnvar", tl - 1))
                s_ac.inc(sc.activation(junk[:], u_t[0][:], AF.Square,
                                       accum_out=stata[:, 0:1]))
                mark(("lnsq", tl), s_ac)

            @blk.vector
            def _(v, tl=tl):
                waitm(v, ("lnu", tl))
                v.reduce_sum(statv[:, 0:1], u_t[0][:], axis=AX)
                waitm(v, ("lnsq", tl))
                v.tensor_scalar_mul(statv[:, 3:4], stata[:, 0:1], 1.0 / D_MODEL)
                v.tensor_scalar_mul(statv[:, 2:3], statv[:, 0:1], 1.0 / D_MODEL)
                v.tensor_scalar_add(statv[:, 3:4], statv[:, 3:4], EPS)
                v.tensor_mul(statv[:, 4:5], statv[:, 2:3], statv[:, 2:3])
                v.tensor_copy(vjunk[:, 512:513], cst[:, 2:3])
                s_ve.inc(v.tensor_sub(statv[:, 5:6], statv[:, 3:4],
                                      statv[:, 4:5]))
                mark(("lnvar", tl), s_ve)

            @blk.scalar
            def _(sc, tl=tl):
                waitm(sc, ("lnvar", tl))
                sc.activation(stata[:, 1:2], statv[:, 5:6], AF.Ln)
                s_ac.inc(sc.activation(stata[:, 1:2], stata[:, 1:2], AF.Exp,
                                       scale=-0.5))
                mark(("lnrstd", tl), s_ac)

            @blk.vector
            def _(v, tl=tl):
                waitm(v, ("lnrstd", tl))
                waitm(v, ("lnst", tl - 1))
                v.tensor_scalar(xnst[0][:], u_t[0][:], statv[:, 2:3],
                                None, ALU.subtract)
                s_ve.inc(v.tensor_scalar_mul(xnst[0][:], xnst[0][:],
                                             stata[:, 1:2]))
                mark(("lnve", tl), s_ve)

            @blk.sync
            def _(q, tl=tl):
                waitm(q, ("lnve", tl))
                s_st.inc(q.dma_start(out=xn_scr[tl * 128:(tl + 1) * 128, :],
                                     in_=xnst[0][:]), 16)
                mark(("lnst", tl), s_st)
        mark(("lnblk", b), s_st)

    def emit_xt_read(b):
        @blk.sync
        def _(q):
            waitm(q, ("lnblk", b))
            waitm(q, ("ipdone", b - 2))
            for k in range(KD):
                i = q.dma_start(
                    out=xnT[b % 2][:, k * 512:(k + 1) * 512],
                    in_=xn_scr[b * 512:(b + 1) * 512, k * 128:(k + 1) * 128],
                    transpose=True)
                s_xt.inc(i, 16)
            mark(("xt", b), s_xt)

    def emit_attn_block(b):
        xb = xnT[b % 2]

        if b > 0:
            @blk.vector
            def _(v):
                waitm(v, ("scores", b * 4 - 1))
                s_ve.inc(v.tensor_copy(b_fm[:, 0:1], b_fm[:, 512:513]))
                mark(("bfmbnd", b), s_ve)

        for j in range(4):
            for gi, (g0, g1) in enumerate((G1, G2, G3)):
                gidx = j * 3 + gi
                bank = B_MISC if gidx % 2 == 0 else B_LD
                w = g1 - g0

                @blk.tensor
                def _(t, j=j, gi=gi, g0=g0, g1=g1, gidx=gidx, bank=bank, w=w):
                    if gidx == 0:
                        waitm(t, ("xt", b))
                    bank_wait(t, bank)
                    for k in range(KD):
                        lhs = xb[:, k * 512 + j * 128:k * 512 + (j + 1) * 128]
                        i = t.matmul(PS[bank][:, 0:w], lhs,
                                     w_inT[:, k * NIP + g0:k * NIP + g1],
                                     start=(k == 0), stop=(k == KD - 1))
                    s_pe.inc(i)
                    mark(("ipg", b, gidx), s_pe)

                if gi == 0:
                    @blk.vector
                    def _(v, j=j, bank=bank):
                        tl = b * 4 + j
                        waitm(v, ("ipg", b, j * 3))
                        waitm(v, ("yfmt", tl - 4))
                        s_ve.inc(v.tensor_copy(
                            xh_bf[:, j * 512:(j + 1) * 512],
                            PS[bank][:, 0:512]))
                        mark(("xhev", b, j), s_ve)
                        bank_touch(bank, s_ve)
                elif gi == 1:
                    @blk.vector
                    def _(v, j=j, bank=bank):
                        tl = b * 4 + j
                        waitm(v, ("ipg", b, j * 3 + 1))
                        waitm(v, ("apath", tl - 4))
                        waitm(v, ("acst", tl - 4))
                        v.tensor_copy(braw[:, j * 128:(j + 1) * 128],
                                      PS[bank][:, 0:128])
                        v.tensor_copy(craw[:, j * 128:(j + 1) * 128],
                                      PS[bank][:, 128:256])
                        o = j * 48
                        v.tensor_add(dtpath[:, o:o + 8], PS[bank][:, 256:264],
                                     dtb_b[:])
                        v.tensor_copy(dtpath[:, o + 40:o + 48],
                                      PS[bank][:, 264:272])
                        s_ve.inc(v.tensor_copy(theta_f[:, j * 64:(j + 1) * 64],
                                               PS[bank][:, 272:336]))
                        mark(("g2ev", b, j), s_ve)
                        bank_touch(bank, s_ve)
                else:
                    @blk.scalar
                    def _(sc, j=j, bank=bank):
                        waitm(sc, ("ipg", b, j * 3 + 2))
                        waitm(sc, ("ygate", b * 4 + j - 4))
                        s_ac.inc(sc.activation(
                            zs_bf[:, j * 512:(j + 1) * 512],
                            PS[bank][:, 0:512], AF.Silu))
                        mark(("zsev", b, j), s_ac)
                        bank_touch(bank, s_ac)

        marks[("ipdone", b)] = marks[("ipg", b, 11)]

        for j in range(4):
            @blk.scalar
            def _(sc, j=j):
                waitm(sc, ("g2ev", b, j))
                o = j * 48
                sc.activation(dtpath[:, o + 16:o + 24], dtpath[:, o:o + 8],
                              AF.Abs)
                sc.activation(dtpath[:, o + 16:o + 24],
                              dtpath[:, o + 16:o + 24], AF.Exp, scale=-1.0)
                sc.activation(dtpath[:, o + 16:o + 24],
                              dtpath[:, o + 16:o + 24], AF.Ln, bias=cst[:, 2:3])
                sc.activation(dtpath[:, o + 8:o + 16], dtpath[:, o:o + 8],
                              AF.Relu)
                s_ac.inc(sc.activation(dtpath[:, o + 32:o + 40],
                                       dtpath[:, o + 40:o + 48], AF.Exp,
                                       scale=-1.0))
                mark(("dtact", b, j), s_ac)

            @blk.vector
            def _(v, j=j):
                tl = b * 4 + j
                waitm(v, ("dtact", b, j))
                waitm(v, ("xhev", b, j))
                waitm(v, ("ssdmm", (tl - 4) * 2 + 1))
                o = j * 48
                v.tensor_scalar_add(dtpath[:, o + 32:o + 40],
                                    dtpath[:, o + 32:o + 40], 1.0)
                v.tensor_add(dtpath[:, o + 8:o + 16], dtpath[:, o + 8:o + 16],
                             dtpath[:, o + 16:o + 24])
                v.reciprocal(dtpath[:, o + 32:o + 40], dtpath[:, o + 32:o + 40])
                v.scalar_tensor_tensor(a_f[:, j * 8:(j + 1) * 8], ealog_b[:],
                                       -1.0, dtpath[:, o + 8:o + 16],
                                       ALU.mult, ALU.mult)
                v.tensor_mul(dtpath[:, o + 24:o + 32], dtpath[:, o + 8:o + 16],
                             dtpath[:, o + 32:o + 40])
                v.tensor_copy(vjunk[:, 512:513], cst[:, 2:3])
                v.tensor_sub(dtpath[:, o + 32:o + 40], dtpath[:, o + 8:o + 16],
                             dtpath[:, o + 24:o + 32])
                xh3 = xh_bf[:, j * 512:(j + 1) * 512].rearrange(
                    "p (h q) -> p h q", h=HPC)
                dl = dtpath[:, o + 24:o + 32]
                dm = dtpath[:, o + 32:o + 40]
                v.tensor_tensor(
                    xd1_bf[:, j * 512:(j + 1) * 512].rearrange(
                        "p (h q) -> p h q", h=HPC),
                    xh3, dl[:, :, None].broadcast_to([128, HPC, HD]), ALU.mult)
                s_ve.inc(v.tensor_tensor(
                    xd2r_bf[:, j * 512:(j + 1) * 512].rearrange(
                        "p (h q) -> p h q", h=HPC),
                    xh3, dm[:, :, None].broadcast_to([128, HPC, HD]), ALU.mult))
                mark(("dtve", b, j), s_ve)

        for j in range(4):
            tl = b * 4 + j

            @blk.tensor
            def _(t, j=j, tl=tl):
                waitm(t, ("dtve", b, j))
                waitm(t, ("apathev", tl - 1))
                bank_wait(t, B_MISC)
                ab = a_f[:, j * 8:(j + 1) * 8]
                t.matmul(PS[B_MISC][:, 128:136], triblk_f[:], ab,
                         start=True, stop=True)
                t.matmul(PS[B_MISC][:, 136:144], triblk_hi_f[:], ab,
                         start=True, stop=True)
                t.matmul(PS[B_MISC][0:1, 144 + j * 16:152 + j * 16],
                         chunkcol_f[:, 0:1], ab, start=True, stop=True)
                t.matmul(PS[B_MISC][0:1, 152 + j * 16:160 + j * 16],
                         chunkcol_f[:, 1:2], ab, start=True, stop=True)
                t.matmul(PS[B_MISC][:, 256:320], triu128_f[:],
                         theta_f[:, j * 64:(j + 1) * 64],
                         start=True, stop=False)
                t.matmul(PS[B_MISC][:, 256:320], ones1_f[:], off_row[:],
                         start=False, stop=True)
                s_pe.inc(t.matmul(PS[B_MISC][0:1, 320:384], onescol_f[:],
                                  theta_f[:, j * 64:(j + 1) * 64],
                                  start=True, stop=True))
                mark(("apath", tl), s_pe)

            @blk.vector
            def _(v, j=j, tl=tl):
                waitm(v, ("apath", tl))
                v.tensor_copy(acs_f[:, j * 8:(j + 1) * 8],
                              PS[B_MISC][:, 128:136])
                v.tensor_copy(ang_f[:, j * 64:(j + 1) * 64],
                              PS[B_MISC][:, 256:320])
                s_ve.inc(v.tensor_add(off_row[:], off_row[:],
                                      PS[B_MISC][0:1, 320:384]))
                mark(("apathev", tl), s_ve)
                bank_touch(B_MISC, s_ve)

            @blk.scalar
            def _(sc, j=j, tl=tl):
                waitm(sc, ("apath", tl))
                waitm(sc, ("mwb", tl - 4))
                waitm(sc, ("ygate", tl - 4))
                waitm(sc, ("decrow", tl - 3, 1))
                sc.activation(dstate_f[:, j * 8:(j + 1) * 8],
                              PS[B_MISC][:, 136:144], AF.Exp)
                sc.activation(eatot_row[0:1, j * 16:(j + 1) * 16],
                              PS[B_MISC][0:1, 144 + j * 16:160 + j * 16],
                              AF.Exp)
                waitm(sc, ("apathev", tl))
                s_ac.inc(sc.activation(eacs_f[:, j * 8:(j + 1) * 8],
                                       acs_f[:, j * 8:(j + 1) * 8], AF.Exp))
                mark(("aact", tl), s_ac)
                bank_touch(B_MISC, s_ac)

        @blk.vector
        def _(v):
            waitm(v, ("apathev", b * 4 + 3))
            v.tensor_scalar_mul(vjunk[:, 0:256], ang_f[:], 1.0 / (2 * PI))
            v.tensor_copy(redk[:], vjunk[:, 0:256])
            v.scalar_tensor_tensor(ang_f[:], redk[:], -2 * PI, ang_f[:],
                                   ALU.mult, ALU.add)
            v.tensor_scalar(vjunk[:, 0:256], ang_f[:], PI, None, ALU.is_gt)
            v.scalar_tensor_tensor(ang_f[:], vjunk[:, 0:256], -2 * PI,
                                   ang_f[:], ALU.mult, ALU.add)
            v.tensor_scalar(vjunk[:, 0:256], ang_f[:], -PI, None, ALU.is_lt)
            s_ve.inc(v.scalar_tensor_tensor(ang_f[:], vjunk[:, 0:256],
                                            2 * PI, ang_f[:],
                                            ALU.mult, ALU.add))
            mark(("angred", b), s_ve)

        @blk.scalar
        def _(sc):
            waitm(sc, ("angred", b))
            waitm(sc, ("rope", b * 4 - 1))
            sc.activation(trig[:, 0:256], ang_f[:], AF.Abs)
            sc.activation(trig[:, 0:256], trig[:, 0:256], AF.Sin,
                          bias=cst[:, 1:2], scale=-1.0)
            s_ac.inc(sc.activation(trig[:, 256:512], ang_f[:], AF.Sin))
            mark(("trig", b), s_ac)

        for j in range(4):
            tl = b * 4 + j

            @blk.tensor
            def _(t, j=j, tl=tl):
                waitm(t, ("aact", tl))
                bank_wait(t, B_PRJ)
                s_pe.inc(t.transpose(PS[B_PRJ][0:8, 0:128],
                                     acs_f[:, j * 8:(j + 1) * 8], ident_f[:]))
                mark(("acst", tl), s_pe)

            @blk.vector
            def _(v, j=j, tl=tl):
                waitm(v, ("acst", tl))
                waitm(v, ("ldmm", tl - 4))
                s_ve.inc(v.tensor_scalar_mul(
                    nacst_f[:, j * 128:(j + 1) * 128],
                    PS[B_PRJ][0:8, 0:128], -1.0))
                mark(("nacst", tl), s_ve)
                bank_touch(B_PRJ, s_ve)

            @blk.gpsimd
            def _(g, j=j, tl=tl):
                waitm(g, ("nacst", tl))
                waitm(g, ("ldmm", tl - 2))
                i = g.dma_start(out=flat_scr[tl % 2],
                                in_=nacst_f[:, j * 128:(j + 1) * 128])
                s_gw.inc(i, 16)
                g.wait_ge(s_gw.h, s_gw.n)
                s_gw.inc(g.dma_start(
                    out=nflat[tl % 2][:].rearrange("p (h s) -> p h s", h=8),
                    in_=flat_scr[tl % 2].rearrange("h (cc s) -> cc h s",
                                                   cc=2)), 16)
                mark(("nflat", tl), s_gw)

            @blk.scalar
            def _(sc, j=j, tl=tl):
                waitm(sc, ("g2ev", b, j))
                waitm(sc, ("rope", tl - 2))
                rb = 4 + (tl % 2) * 2
                sc.activation(junk[:, 0:128], braw[:, j * 128:(j + 1) * 128],
                              AF.Square, accum_out=stata[:, rb:rb + 1])
                sc.activation(junk[:, 128:256], craw[:, j * 128:(j + 1) * 128],
                              AF.Square, accum_out=stata[:, rb + 1:rb + 2])
                sc.activation(stata[:, rb:rb + 1], stata[:, rb:rb + 1], AF.Ln,
                              scale=1.0 / DSTATE, bias=cst[:, 3:4])
                sc.activation(stata[:, rb + 1:rb + 2], stata[:, rb + 1:rb + 2],
                              AF.Ln, scale=1.0 / DSTATE, bias=cst[:, 3:4])
                sc.activation(stata[:, rb:rb + 1], stata[:, rb:rb + 1], AF.Exp,
                              scale=-0.5)
                s_ac.inc(sc.activation(stata[:, rb + 1:rb + 2],
                                       stata[:, rb + 1:rb + 2], AF.Exp,
                                       scale=-0.5))
                mark(("bcrstd", tl), s_ac)

            @blk.vector
            def _(v, j=j, tl=tl):
                waitm(v, ("bcrstd", tl))
                waitm(v, ("trig", b))
                waitm(v, ("stmm", (tl - 4) * 2 + 1))
                waitm(v, ("shift", tl - 4))
                rb = 4 + (tl % 2) * 2
                cosj = trig[:, j * 64:(j + 1) * 64]
                sinj = trig[:, 256 + j * 64:256 + (j + 1) * 64]
                for (raw, nb, wt, rc) in ((braw, bn_bf, bnw_b, rb),
                                          (craw, cn_bf, cnw_b, rb + 1)):
                    sl = slice(j * 128, (j + 1) * 128)
                    v.tensor_mul(vjunk[:, 0:128], raw[:, sl], wt[:])
                    v.tensor_scalar_mul(vjunk[:, 0:128], vjunk[:, 0:128],
                                        stata[:, rc:rc + 1])
                    v0 = vjunk[:, 0:128:2]
                    v1 = vjunk[:, 1:128:2]
                    v.tensor_mul(vjunk[:, 256:320], cosj, v0)
                    v.tensor_mul(vjunk[:, 320:384], sinj, v1)
                    v.tensor_mul(vjunk[:, 384:448], sinj, v0)
                    v.tensor_mul(vjunk[:, 448:512], cosj, v1)
                    v.tensor_sub(nb[:, j * 128:(j + 1) * 128:2],
                                 vjunk[:, 256:320], vjunk[:, 320:384])
                    i = v.tensor_add(nb[:, j * 128 + 1:(j + 1) * 128:2],
                                     vjunk[:, 384:448], vjunk[:, 448:512])
                s_ve.inc(i)
                mark(("rope", tl), s_ve)

            @blk.scalar
            def _(sc, j=j, tl=tl):
                waitm(sc, ("rope", tl))
                waitm(sc, ("dtve", b, j))
                waitm(sc, ("stmm", (tl - 4) * 2 + 1))
                sl = slice(j * 128, (j + 1) * 128)
                s_sh.inc(sc.dma_start(out=b2_bf[0:1, sl],
                                      in_=bprev[tl % 2][:]), 16)
                s_sh.inc(sc.dma_start(out=b2_bf[1:128, sl],
                                      in_=bn_bf[0:127, sl]), 16)
                s_sh.inc(sc.dma_start(out=bprev[(tl + 1) % 2][:],
                                      in_=bn_bf[127:128, sl]), 16)
                sl2 = slice(j * 512, (j + 1) * 512)
                s_sh.inc(sc.dma_start(out=xd2_bf[0:1, sl2],
                                      in_=xprev[tl % 2][:]), 16)
                s_sh.inc(sc.dma_start(out=xd2_bf[1:128, sl2],
                                      in_=xd2r_bf[0:127, sl2]), 16)
                s_sh.inc(sc.dma_start(out=xprev[(tl + 1) % 2][:],
                                      in_=xd2r_bf[127:128, sl2]), 16)
                mark(("shift", tl), s_sh)

            @blk.tensor
            def _(t, j=j, tl=tl):
                waitm(t, ("rope", tl))
                bank_wait(t, B_PRJ)
                bp = PS[B_PRJ].bitcast(BF16)
                t.transpose(bp[:, 0:128], bn_bf[:, j * 128:(j + 1) * 128],
                            ident_bf[:])
                s_pe.inc(t.transpose(bp[:, 128:256],
                                     cn_bf[:, j * 128:(j + 1) * 128],
                                     ident_bf[:]))
                mark(("bcfm", tl), s_pe)

            @blk.vector
            def _(v, j=j, tl=tl):
                waitm(v, ("bcfm", tl))
                waitm(v, ("ssdmm", (tl - 4) * 2 + 1))
                bp = PS[B_PRJ].bitcast(BF16)
                v.tensor_copy(b_fm[:, 1 + j * 128:1 + (j + 1) * 128],
                              bp[:, 0:128])
                s_ve.inc(v.tensor_copy(c_fm[:, j * 128:(j + 1) * 128],
                                       bp[:, 128:256]))
                mark(("bcfmev", tl), s_ve)
                bank_touch(B_PRJ, s_ve)

            @blk.tensor
            def _(t, j=j, tl=tl):
                waitm(t, ("nflat", tl))
                bank_wait(t, B_LD)
                t.matmul(PS[B_LD][:, 0:512], ind2neg_f[:], nflat[tl % 2][:],
                         start=True, stop=False)
                s_pe.inc(t.matmul(PS[B_LD][:, 0:512],
                                  nacst_f[:, j * 128:(j + 1) * 128],
                                  delta8_f[:], start=False, stop=True))
                mark(("ldmm", tl), s_pe)

            @blk.vector
            def _(v, j=j, tl=tl):
                waitm(v, ("ldmm", tl))
                s_ve.inc(v.tensor_add(sctmp[:], PS[B_LD][:, 0:512],
                                      tribias_f[:]))
                mark(("ldadd", tl), s_ve)
                bank_touch(B_LD, s_ve)

            @blk.scalar
            def _(sc, j=j, tl=tl):
                waitm(sc, ("ldadd", tl))
                waitm(sc, ("mwb", tl - 2))
                s_ac.inc(sc.activation(ld_bf[tl % 2][:], sctmp[:], AF.Exp))
                mark(("ldiff", tl), s_ac)

            @blk.tensor
            def _(t, j=j, tl=tl):
                waitm(t, ("bcfmev", tl))
                waitm(t, ("shift", tl))
                bank_wait(t, B_MISC)
                for rho in range(2):
                    c0 = 1 + j * 128 + rho * 64
                    csl = slice(j * 128 + rho * 64, j * 128 + (rho + 1) * 64)
                    t.matmul(PS[B_MISC][rho * 64:(rho + 1) * 64, 0:64],
                             b_fm[:, c0:c0 + 64], c_fm[:, csl],
                             start=True, stop=True)
                    i = t.matmul(PS[B_MISC][rho * 64:(rho + 1) * 64, 64:128],
                                 b_fm[:, c0 - 1:c0 + 63], c_fm[:, csl],
                                 start=True, stop=True)
                s_pe.inc(i)
                mark(("scores", tl), s_pe)

            @blk.vector
            def _(v, j=j, tl=tl):
                waitm(v, ("scores", tl))
                waitm(v, ("ldiff", tl))
                waitm(v, ("ssdmm", tl * 2 - 1))
                for rho in range(2):
                    r = slice(rho * 64, (rho + 1) * 64)
                    for ssd in range(2):
                        sc_ap = PS[B_MISC][r, ssd * 64:(ssd + 1) * 64]
                        v.tensor_tensor(
                            mw_bf[r, ssd * 512:(ssd + 1) * 512].rearrange(
                                "p (h q) -> p h q", h=HPC),
                            sc_ap[:, None, :].broadcast_to([64, HPC, 64]),
                            ld_bf[tl % 2][r, :].rearrange(
                                "p (h q) -> p h q", h=HPC),
                            ALU.mult)
                ds3 = dstate_f[:, j * 8:(j + 1) * 8]
                v.tensor_tensor(
                    xds_bf[:, 0:512].rearrange("p (h q) -> p h q", h=HPC),
                    xd1_bf[:, j * 512:(j + 1) * 512].rearrange(
                        "p (h q) -> p h q", h=HPC),
                    ds3[:, :, None].broadcast_to([128, HPC, HD]), ALU.mult)
                s_ve.inc(v.tensor_tensor(
                    xds_bf[:, 512:1024].rearrange("p (h q) -> p h q", h=HPC),
                    xd2_bf[:, j * 512:(j + 1) * 512].rearrange(
                        "p (h q) -> p h q", h=HPC),
                    ds3[:, :, None].broadcast_to([128, HPC, HD]), ALU.mult))
                mark(("mwb", tl), s_ve)
                bank_touch(B_MISC, s_ve)

            @blk.vector
            def _(v, j=j, tl=tl):
                waitm(v, ("aact", tl))
                waitm(v, ("stmm", tl * 2 - 1))
                s_ve.inc(v.tensor_copy(
                    decrow_bf[0:1, :].rearrange("p (h q) -> p h q", h=HPC),
                    eatot_row[0:1, j * 16:j * 16 + 8]
                    [:, :, None].broadcast_to([1, HPC, HD])))
                mark(("decrow", tl, 0), s_ve)

            for rho in range(2):
                c = tl * 2 + rho

                @blk.tensor
                def _(t, j=j, tl=tl, rho=rho, c=c):
                    waitm(t, ("mwb", tl))
                    r = slice(rho * 64, (rho + 1) * 64)
                    bank_wait(t, B_S1)
                    bank_wait(t, B_S2)
                    t.matmul(PS[B_S1][:, :], bn_bf[r, j * 128:(j + 1) * 128],
                             xds_bf[r, 0:512], start=True, stop=True)
                    t.matmul(PS[B_S2][:, :], b2_bf[r, j * 128:(j + 1) * 128],
                             xds_bf[r, 512:1024], start=True, stop=True)
                    waitm(t, ("decrow", tl, rho))
                    bank_wait(t, B_DEC)
                    i = t.matmul(PS[B_DEC][:, :], ones1_bf[:],
                                 decrow_bf[0:1, :], start=True, stop=True)
                    s_pe.inc(i)
                    mark(("stmm", c), s_pe)
                    bank_wait(t, B_YD)
                    for h in range(HPC):
                        for ssd in range(2):
                            xs_ap = xd1_bf if ssd == 0 else xd2_bf
                            t.matmul(
                                PS[B_YD][r, h * 64:(h + 1) * 64],
                                mw_bf[r, ssd * 512 + h * 64:
                                      ssd * 512 + (h + 1) * 64],
                                xs_ap[r, j * 512 + h * 64:
                                      j * 512 + (h + 1) * 64],
                                start=(ssd == 0), stop=(ssd == 1))
                    waitm(t, ("interbf", c))
                    bank_wait(t, B_YO)
                    csl = slice(j * 128 + rho * 64, j * 128 + (rho + 1) * 64)
                    t.matmul(PS[B_YO][r, :], c_fm[:, csl], inter_bf[0][:],
                             start=True, stop=False)
                    i = t.matmul(PS[B_YO][r, :], c_fm[:, csl], inter_bf[1][:],
                                 start=False, stop=True)
                    s_pe.inc(i)
                    mark(("ssdmm", c), s_pe)

                @blk.vector
                def _(v, j=j, tl=tl, rho=rho, c=c):
                    waitm(v, ("stmm", c))
                    for ssd in range(2):
                        bank = PS[B_S1] if ssd == 0 else PS[B_S2]
                        v.tensor_tensor(sctmp[:], inter[ssd][:], PS[B_DEC][:],
                                        ALU.mult)
                        v.tensor_add(inter[ssd][:], sctmp[:], bank[:])
                    s_ve.inc(v.engine_nop())
                    mark(("scan", c), s_ve)
                    bank_touch(B_S1, s_ve)
                    bank_touch(B_S2, s_ve)
                    bank_touch(B_DEC, s_ve)
                    waitm(v, ("ssdmm", c))
                    v.tensor_copy(inter_bf[0][:], inter[0][:])
                    s_ve.inc(v.tensor_copy(inter_bf[1][:], inter[1][:]))
                    mark(("interbf", c + 1), s_ve)
                    if rho == 0:
                        s_ve.inc(v.tensor_copy(
                            decrow_bf[0:1, :].rearrange("p (h q) -> p h q",
                                                        h=HPC),
                            eatot_row[0:1, j * 16 + 8:j * 16 + 16]
                            [:, :, None].broadcast_to([1, HPC, HD])))
                        mark(("decrow", tl, 1), s_ve)

            @blk.vector
            def _(v, j=j, tl=tl):
                waitm(v, ("ssdmm", tl * 2 + 1))
                waitm(v, ("zsev", b, j))
                e3 = eacs_f[:, j * 8:(j + 1) * 8]
                v.tensor_tensor(
                    ytmp[:].rearrange("p (h q) -> p h q", h=HPC),
                    PS[B_YO][:].rearrange("p (h q) -> p h q", h=HPC),
                    e3[:, :, None].broadcast_to([128, HPC, HD]), ALU.mult)
                v.tensor_add(y_f[:], PS[B_YD][:], ytmp[:])
                bank_touch(B_YD, s_ve)
                bank_touch(B_YO, s_ve)
                v.tensor_mul(ytmp[:], xh_bf[:, j * 512:(j + 1) * 512], d_b[:])
                v.tensor_add(y_f[:], y_f[:], ytmp[:])
                s_ve.inc(v.tensor_mul(yg_bf[:, j * 512:(j + 1) * 512], y_f[:],
                                      zs_bf[:, j * 512:(j + 1) * 512]))
                mark(("ygate", tl), s_ve)

            if "yg" in dbg_d or "y" in dbg_d:
                @blk.sync
                def _(q, j=j, tl=tl):
                    waitm(q, ("ygate", tl))
                    if "y" in dbg_d:
                        s_db.inc(q.dma_start(
                            out=dbg_d["y"][tl * 128:(tl + 1) * 128, :],
                            in_=y_f[:]), 16)
                    if "yg" in dbg_d:
                        s_db.inc(q.dma_start(
                            out=dbg_d["yg"][tl * 128:(tl + 1) * 128, :],
                            in_=yg_bf[:, j * 512:(j + 1) * 512]), 16)
                    mark(("dbgyg", tl), s_db)

            @blk.scalar
            def _(sc, j=j, tl=tl):
                waitm(sc, ("ygate", tl))
                s_ac.inc(sc.activation(junk[:, 0:512],
                                       yg_bf[:, j * 512:(j + 1) * 512],
                                       AF.Square, accum_out=ssq[:, tl:tl + 1]))
                mark(("ssqt", tl), s_ac)

            @blk.tensor
            def _(t, j=j, tl=tl):
                waitm(t, ("ygate", tl))
                if ("dbgyg", tl) in marks:
                    waitm(t, ("dbgyg", tl))
                bank_wait(t, B_PRJ)
                bp = PS[B_PRJ].bitcast(BF16)
                for q in range(4):
                    i = t.transpose(
                        bp[:, q * 128:(q + 1) * 128],
                        yg_bf[:, j * 512 + q * 128:j * 512 + (q + 1) * 128],
                        ident_bf[:])
                s_pe.inc(i)
                mark(("yfmt", tl), s_pe)

            @blk.vector
            def _(v, j=j, tl=tl):
                waitm(v, ("yfmt", tl))
                waitm(v, ("opmm", tl - 4, 3))
                bp = PS[B_PRJ].bitcast(BF16)
                for q in range(4):
                    i = v.tensor_copy(y_fm[:, q * 512 + j * 128:
                                           q * 512 + (j + 1) * 128],
                                      bp[:, q * 128:(q + 1) * 128])
                s_ve.inc(i)
                mark(("yfm", tl), s_ve)
                bank_touch(B_PRJ, s_ve)

        for j in range(4):
            tl = b * 4 + j

            for g in range(4):
                bank = B_DEC if g % 2 == 0 else B_PRJ

                @blk.tensor
                def _(t, j=j, tl=tl, g=g, bank=bank):
                    if g == 0:
                        waitm(t, ("yfm", b * 4 + 3))
                    bank_wait(t, bank)
                    for k in range(4):
                        i = t.matmul(
                            PS[bank][:, :],
                            y_fm[:, k * 512 + j * 128:k * 512 + (j + 1) * 128],
                            w_outT[:, k * D_MODEL + g * 512:
                                   k * D_MODEL + (g + 1) * 512],
                            start=(k == 0), stop=(k == 3))
                    s_pe.inc(i)
                    mark(("opmm", tl, g), s_pe)

                @blk.vector
                def _(v, g=g, bank=bank, tl=tl):
                    waitm(v, ("opmm", tl, g))
                    if g == 0:
                        waitm(v, ("stgdma", tl - 1))
                    i = v.tensor_copy(
                        stage[0][:, g * 512:(g + 1) * 512],
                        PS[bank][:, :])
                    s_ve.inc(i)
                    mark(("opev", tl, g), s_ve)
                    bank_touch(bank, s_ve)

            @blk.sync
            def _(q, tl=tl):
                waitm(q, ("opev", tl, 3))
                s_st.inc(q.dma_start(out=cc_in[tl * 128:(tl + 1) * 128, :],
                                     in_=stage[0][:]), 16)
                mark(("stgdma", tl), s_st)

    emit_ln(0)
    if NB > 1:
        emit_ln(1)
    if phases != "ln":
        emit_xt_read(0)
    for b in range(NB):
        if b + 2 < NB:
            emit_ln(b + 2)
        if phases == "ln":
            continue
        if b + 1 < NB:
            emit_xt_read(b + 1)
        emit_attn_block(b)

    if phases == "ln":
        if "drams" in dbg:
            @blk.sync
            def _(q):
                q.wait_ge(s_st.h, s_st.n)
                s_db.inc(q.dma_start(out=dbg_d["xn"][:], in_=xn_scr[:]), 16)

        @blk.sync
        def _(q):
            q.wait_ge(s_st.h, s_st.n)
            if s_db.n:
                q.wait_ge(s_db.h, s_db.n)
        attn_es.close()
        es.close()
        nc.finalize()
        return nc

    @blk.tensor
    def _(t):
        waitm(t, ("ssqt", NT - 1))
        bank_wait(t, B_MISC)
        s_pe.inc(t.transpose(PS[B_MISC][0:min(NT, 128), 0:128],
                             ssq[:, 0:NT], ident_f[:]))
        mark("ssqT", s_pe)

    @blk.vector
    def _(v):
        waitm(v, "ssqT")
        s_ve.inc(v.tensor_copy(junk[0:NT, 0:128], PS[B_MISC][0:NT, 0:128]))
        mark("ssqTev", s_ve)
        bank_touch(B_MISC, s_ve)

    @blk.sync
    def _(q):
        waitm(q, "ssqTev")
        s_st.inc(q.dma_start(
            out=cc2_in.rearrange("(t p) one -> t (p one)", p=128),
            in_=junk[0:NT, 0:128]), 16)
        mark("cc2st", s_st)

    @blk.gpsimd
    def _(g):
        g.wait_ge(s_st.h, s_st.n)
        if sim_cc_bypass:
            g.dma_start(out=cc_out[:], in_=cc_in[0:TOKC, :]).then_inc(s_cc.h, 16)
            g.dma_start(out=cc2_out[:], in_=cc2_in[0:TOKC, :]).then_inc(
                s_cc.h, 16)
            s_cc.n += 32
        else:
            i1 = g.collective_compute(
                "ReduceScatter", ALU.add, replica_groups=[list(range(NCORE))],
                ins=[cc_in[:]], outs=[cc_out[:]])
            s_cc.inc(i1, 1)
            i2 = g.collective_compute(
                "ReduceScatter", ALU.add, replica_groups=[list(range(NCORE))],
                ins=[cc2_in[:]], outs=[cc2_out[:]])
            s_cc.inc(i2, 1)
    mark("ccdone", s_cc)

    nc.all_engine_barrier()
    attn_es.close()

    f1_es = ExitStack()
    uo_t = sb("uo_t", 128, D_MODEL, F32, f1_es)
    xno_t = sb("xno_t", 128, D_MODEL, BF16, f1_es)
    fstatv = sb("fstatv", 128, 8, F32, f1_es)
    fstata = sb("fstata", 128, 4, F32, f1_es)

    for tt in range(NTO):
        @blk.sync
        def _(q, tt=tt):
            waitm(q, ("olnst", tt - 1))
            s_u.inc(q.dma_start(out=uo_t[:],
                                in_=uown_d[tt * 128:(tt + 1) * 128, :]), 16)
            mark(("olnu", tt), s_u)

        @blk.scalar
        def _(sc, tt=tt):
            waitm(sc, ("olnu", tt))
            waitm(sc, ("olnvar", tt - 1))
            waitm(sc, "cc2st")
            s_ac.inc(sc.activation(junk[:], uo_t[:], AF.Square,
                                   accum_out=fstata[:, 0:1]))
            mark(("olnsq", tt), s_ac)

        @blk.vector
        def _(v, tt=tt):
            waitm(v, ("olnu", tt))
            v.reduce_sum(fstatv[:, 0:1], uo_t[:], axis=AX)
            waitm(v, ("olnsq", tt))
            v.tensor_scalar_mul(fstatv[:, 3:4], fstata[:, 0:1], 1.0 / D_MODEL)
            v.tensor_scalar_mul(fstatv[:, 2:3], fstatv[:, 0:1], 1.0 / D_MODEL)
            v.tensor_scalar_add(fstatv[:, 3:4], fstatv[:, 3:4], EPS)
            v.tensor_mul(fstatv[:, 4:5], fstatv[:, 2:3], fstatv[:, 2:3])
            v.tensor_copy(vjunk[:, 512:513], cst[:, 2:3])
            s_ve.inc(v.tensor_sub(fstatv[:, 5:6], fstatv[:, 3:4],
                                  fstatv[:, 4:5]))
            mark(("olnvar", tt), s_ve)

        @blk.scalar
        def _(sc, tt=tt):
            waitm(sc, ("olnvar", tt))
            sc.activation(fstata[:, 1:2], fstatv[:, 5:6], AF.Ln)
            s_ac.inc(sc.activation(fstata[:, 1:2], fstata[:, 1:2], AF.Exp,
                                   scale=-0.5))
            mark(("olnrstd", tt), s_ac)

        @blk.vector
        def _(v, tt=tt):
            waitm(v, ("olnrstd", tt))
            v.tensor_scalar(xno_t[:], uo_t[:], fstatv[:, 2:3], None,
                            ALU.subtract)
            s_ve.inc(v.tensor_scalar_mul(xno_t[:], xno_t[:], fstata[:, 1:2]))
            mark(("olnve", tt), s_ve)

        @blk.sync
        def _(q, tt=tt):
            waitm(q, ("olnve", tt))
            s_st.inc(q.dma_start(out=xno_scr[tt * 128:(tt + 1) * 128, :],
                                 in_=xno_t[:]), 16)
            mark(("olnst", tt), s_st)

    nc.all_engine_barrier()
    f1_es.close()

    f2_es = ExitStack()
    xnT_own = sb("xnT_own", 128, KD * TOKC, BF16, f2_es)
    gu_s = [sb("gu_s%d" % i, 128, KD * 128, BF16, f2_es) for i in range(2)]
    dp_s = [sb("dp_s%d" % i, 128, 32 * 128, BF16, f2_es) for i in range(2)]
    h_fm = sb("h_fm", 128, 32 * TOKC, BF16, f2_es)
    silu_t = sb("silu_t", 128, TOKC, BF16, f2_es)
    probs_s = sb("probs_s", 128, NTO * 8, F32, f2_es)
    wexp_s = sb("wexp_s", 128, NTO * 8, F32, f2_es)
    wrow_bf = sb("wrow_bf", 8, NTO * 128, BF16, f2_es)
    wb_bf = sb("wb_bf", 128, TOKC, BF16, f2_es)
    f2stat = sb("f2stat", 128, 8 * max(8, 1), F32, f2_es)
    fstage = [sb("fstage%d" % i, 128, TOKC, F32, f2_es) for i in range(2)]

    @blk.sync
    def _(q):
        waitm(q, ("olnst", NTO - 1))
        for k in range(KD):
            i = q.dma_start(out=xnT_own[:, k * TOKC:(k + 1) * TOKC],
                            in_=xno_scr[:, k * 128:(k + 1) * 128],
                            transpose=True)
            s_xt.inc(i, 16)
        mark("xtown", s_xt)

    @blk.tensor
    def _(t):
        waitm(t, "xtown")
        bank_wait(t, B_MISC)
        for tt in range(NTO):
            for k in range(KD):
                i = t.matmul(PS[B_MISC][:, tt * 8:(tt + 1) * 8],
                             xnT_own[:, k * TOKC + tt * 128:
                                     k * TOKC + (tt + 1) * 128],
                             gwT_s[:, k * 8:(k + 1) * 8],
                             start=(k == 0), stop=(k == KD - 1))
        s_pe.inc(i)
        mark("router", s_pe)

    @blk.vector
    def _(v):
        waitm(v, "router")
        for tt in range(NTO):
            v.reduce_max(f2stat[:, tt * 8:tt * 8 + 1],
                         PS[B_MISC][:, tt * 8:(tt + 1) * 8], axis=AX)
        for tt in range(NTO):
            sl = slice(tt * 8, (tt + 1) * 8)
            v.tensor_scalar(probs_s[:, sl], PS[B_MISC][:, sl],
                            f2stat[:, tt * 8:tt * 8 + 1], None, ALU.subtract)
        s_ve.inc(v.engine_nop())
        mark("rsub", s_ve)
        bank_touch(B_MISC, s_ve)

    @blk.scalar
    def _(sc):
        waitm(sc, "rsub")
        s_ac.inc(sc.activation(probs_s[:], probs_s[:], AF.Exp))
        mark("rexp", s_ac)

    @blk.vector
    def _(v):
        waitm(v, "rexp")
        SL = [slice(tt * 8, (tt + 1) * 8) for tt in range(NTO)]
        c = lambda tt, k: f2stat[:, tt * 8 + k:tt * 8 + k + 1]
        for tt in range(NTO):
            v.reduce_sum(c(tt, 1), probs_s[:, SL[tt]], axis=AX)
        for tt in range(NTO):
            v.reciprocal(c(tt, 2), c(tt, 1))
        for tt in range(NTO):
            v.tensor_scalar_mul(probs_s[:, SL[tt]], probs_s[:, SL[tt]],
                                c(tt, 2))
        for tt in range(NTO):
            v.reduce_max(c(tt, 3), probs_s[:, SL[tt]], axis=AX)
        for tt in range(NTO):
            v.tensor_scalar(wexp_s[:, SL[tt]], probs_s[:, SL[tt]], c(tt, 3),
                            None, ALU.is_ge)
        for tt in range(NTO):
            v.scalar_tensor_tensor(vjunk[:, tt * 8:(tt + 1) * 8],
                                   wexp_s[:, SL[tt]], BIGNEG,
                                   probs_s[:, SL[tt]], ALU.mult, ALU.add)
        for tt in range(NTO):
            v.reduce_max(c(tt, 4), vjunk[:, tt * 8:(tt + 1) * 8], axis=AX)
        for tt in range(NTO):
            v.tensor_scalar(wexp_s[:, SL[tt]], probs_s[:, SL[tt]], c(tt, 4),
                            None, ALU.is_ge)
        for tt in range(NTO):
            v.tensor_add(c(tt, 5), c(tt, 3), c(tt, 4))
        for tt in range(NTO):
            v.reciprocal(c(tt, 6), c(tt, 5))
        for tt in range(NTO):
            v.tensor_mul(wexp_s[:, SL[tt]], wexp_s[:, SL[tt]],
                         probs_s[:, SL[tt]])
        for tt in range(NTO):
            v.tensor_scalar_mul(wexp_s[:, SL[tt]], wexp_s[:, SL[tt]],
                                c(tt, 6))
        s_ve.inc(v.engine_nop())
        mark("topk", s_ve)

    if "wexp" in dbg_d:
        @blk.sync
        def _(q):
            waitm(q, "topk")
            s_db.inc(q.dma_start(
                out=dbg_d["wexp"].rearrange("(t p) e -> p (t e)", p=128),
                in_=wexp_s[:]), 16)

    @blk.tensor
    def _(t):
        waitm(t, "topk")
        bank_wait(t, B_LD)
        bank_wait(t, B_MISC)
        for tt in range(NTO):
            bank = B_LD if tt < 4 else B_MISC
            i = t.transpose(PS[bank][0:8, (tt % 4) * 128:(tt % 4 + 1) * 128],
                            wexp_s[:, tt * 8:(tt + 1) * 8], ident_f[:])
        s_pe.inc(i)
        mark("wrowT", s_pe)

    @blk.vector
    def _(v):
        waitm(v, "wrowT")
        for tt in range(NTO):
            bank = B_LD if tt < 4 else B_MISC
            i = v.tensor_copy(wrow_bf[:, tt * 128:(tt + 1) * 128],
                              PS[bank][0:8, (tt % 4) * 128:(tt % 4 + 1) * 128])
        s_ve.inc(i)
        mark("wrow", s_ve)
        bank_touch(B_LD, s_ve)
        bank_touch(B_MISC, s_ve)

    for e in range(NEXP):
        for hi_, (h0_, hw_) in enumerate(HALVES):
            @blk.tensor
            def _(t, e=e, hi_=hi_, h0_=h0_, hw_=hw_):
                if hi_ == 0:
                    waitm(t, "wrow")
                    waitm(t, ("dpmm", e - 1, 15))
                bank_wait(t, B_DEC)
                i = t.matmul(PS[B_DEC][:, 0:hw_],
                             deltacol_bf[:, e * 128:(e + 1) * 128],
                             wrow_bf[:, h0_:h0_ + hw_],
                             start=True, stop=True)
                s_pe.inc(i)
                mark(("wbmm", e, hi_), s_pe)

            @blk.vector
            def _(v, e=e, hi_=hi_, h0_=h0_, hw_=hw_):
                waitm(v, ("wbmm", e, hi_))
                i = v.tensor_copy(wb_bf[:, h0_:h0_ + hw_],
                                  PS[B_DEC][:, 0:hw_])
                s_ve.inc(i)
                mark(("wbev", e, hi_), s_ve)
                bank_touch(B_DEC, s_ve)

        for m in range(32):
            for part, mm in ((0, m), (1, m + 32)):
                @blk.gpsimd
                def _(g, e=e, m=m, part=part, mm=mm):
                    waitm(g, ("gumm", e, m - 1, part))
                    i = g.dma_start(out=gu_s[part][:], in_=guT_d[e, mm])
                    s_gw.inc(i, 16)
                    mark(("guld", e, m, part), s_gw)

                @blk.tensor
                def _(t, e=e, m=m, part=part, mm=mm):
                    waitm(t, ("guld", e, m, part))
                    banks = (B_S1, B_YD) if part == 0 else (B_S2, B_YO)
                    for half in range(NHALF):
                        bank_wait(t, banks[half])
                    for k in range(KD):
                        for half, (h0_, hw_) in enumerate(HALVES):
                            i = t.matmul(
                                PS[banks[half]][:, 0:hw_],
                                gu_s[part][:, k * 128:(k + 1) * 128],
                                xnT_own[:, k * TOKC + h0_:
                                        k * TOKC + h0_ + hw_],
                                start=(k == 0), stop=(k == KD - 1))
                    s_pe.inc(i)
                    mark(("gumm", e, m, part), s_pe)

            @blk.scalar
            def _(sc, e=e, m=m):
                waitm(sc, ("gumm", e, m, 0))
                waitm(sc, ("hev", e, m - 1))
                for half, (h0_, hw_) in enumerate(HALVES):
                    bank = PS[B_S1] if half == 0 else PS[B_YD]
                    i = sc.activation(silu_t[:, h0_:h0_ + hw_],
                                      bank[:, 0:hw_], AF.Silu)
                s_ac.inc(i)
                mark(("gsilu", e, m), s_ac)
                bank_touch(B_S1, s_ac)
                if NHALF > 1:
                    bank_touch(B_YD, s_ac)

            @blk.vector
            def _(v, e=e, m=m):
                waitm(v, ("gsilu", e, m))
                waitm(v, ("gumm", e, m, 1))
                waitm(v, ("dpmm", e - 1, 15))
                for half, (h0_, hw_) in enumerate(HALVES):
                    bank = PS[B_S2] if half == 0 else PS[B_YO]
                    i = v.tensor_mul(
                        h_fm[:, m * TOKC + h0_:m * TOKC + h0_ + hw_],
                        silu_t[:, h0_:h0_ + hw_], bank[:, 0:hw_])
                s_ve.inc(i)
                mark(("hev", e, m), s_ve)
                bank_touch(B_S2, s_ve)
                if NHALF > 1:
                    bank_touch(B_YO, s_ve)

        for m in range(16):
            @blk.gpsimd
            def _(g, e=e, m=m):
                waitm(g, ("dpmm", e, m - 2))
                i = g.dma_start(out=dp_s[m % 2][:], in_=dpT_d[e, m])
                s_gw.inc(i, 16)
                mark(("dpld", e, m), s_gw)

            @blk.tensor
            def _(t, e=e, m=m):
                waitm(t, ("dpld", e, m))
                waitm(t, ("hev", e, 31))
                banks = (B_LD, B_S1) if m % 2 == 0 else (B_MISC, B_S2)
                for half in range(NHALF):
                    bank_wait(t, banks[half])
                for k in range(32):
                    for half, (h0_, hw_) in enumerate(HALVES):
                        i = t.matmul(
                            PS[banks[half]][:, 0:hw_],
                            dp_s[m % 2][:, k * 128:(k + 1) * 128],
                            h_fm[:, k * TOKC + h0_:k * TOKC + h0_ + hw_],
                            start=(k == 0), stop=(k == 31))
                s_pe.inc(i)
                mark(("dpmm", e, m), s_pe)

            @blk.vector
            def _(v, e=e, m=m):
                waitm(v, ("dpmm", e, m))
                waitm(v, ("wbev", e, NHALF - 1))
                waitm(v, ("fstdma", e, m - 2))
                banks = (B_LD, B_S1) if m % 2 == 0 else (B_MISC, B_S2)
                for half, (h0_, hw_) in enumerate(HALVES):
                    i = v.tensor_mul(
                        fstage[m % 2][:, h0_:h0_ + hw_],
                        PS[banks[half]][:, 0:hw_],
                        wb_bf[:, h0_:h0_ + hw_])
                s_ve.inc(i)
                mark(("dpev", e, m), s_ve)
                for half in range(NHALF):
                    bank_touch(banks[half], s_ve)

            @blk.sync
            def _(q, e=e, m=m):
                waitm(q, ("dpev", e, m))
                s_st.inc(q.dma_start(out=gpart[e, m * 128:(m + 1) * 128, :],
                                     in_=fstage[m % 2][:]), 16)
                mark(("fstdma", e, m), s_st)

    nc.all_engine_barrier()
    f2_es.close()

    f3_es = ExitStack()
    gld = [sb("gld%d" % i, 128, TOKC, F32, f3_es) for i in range(2)]
    gacc = sb("gacc", 128, TOKC, F32, f3_es)
    gs_bf = sb("gs_bf", 128, TOKC, BF16, f3_es)
    s_t = sb("s_t", 128, D_MODEL, F32, f3_es)
    g_t = sb("g_t", 128, D_MODEL, BF16, f3_es)
    uo2_t = sb("uo2_t", 128, D_MODEL, F32, f3_es)
    o_t = sb("o_t", 128, D_MODEL, F32, f3_es)
    rstat = sb("rstat", 128, 4, F32, f3_es)

    for m in range(16):
        for e in range(NEXP):
            @blk.sync
            def _(q, m=m, e=e):
                waitm(q, ("fstdma", NEXP - 1, m))
                waitm(q, ("gaccv", m, e - 2))
                s_u.inc(q.dma_start(out=gld[e % 2][:],
                                    in_=gpart[e, m * 128:(m + 1) * 128, :]),
                        16)
                mark(("gldd", m, e), s_u)

            @blk.vector
            def _(v, m=m, e=e):
                waitm(v, ("gldd", m, e))
                if e == 0:
                    i = v.tensor_copy(gacc[:], gld[0][:])
                else:
                    i = v.tensor_add(gacc[:], gacc[:], gld[e % 2][:])
                s_ve.inc(i)
                mark(("gaccv", m, e), s_ve)

        @blk.scalar
        def _(sc, m=m):
            waitm(sc, ("gaccv", m, NEXP - 1))
            waitm(sc, ("gsumst", m - 1))
            s_ac.inc(sc.activation(gs_bf[:], gacc[:], AF.Silu))
            mark(("gsilu2", m), s_ac)

        @blk.sync
        def _(q, m=m):
            waitm(q, ("gsilu2", m))
            s_st.inc(q.dma_start(out=gsum_d[m * 128:(m + 1) * 128, :],
                                 in_=gs_bf[:]), 16)
            mark(("gsumst", m), s_st)

    for tt in range(NTO):
        @blk.sync
        def _(q, tt=tt):
            if tt == 0:
                q.wait_ge(s_cc.h, s_cc.n)
            waitm(q, ("gsumst", 15))
            waitm(q, ("fincomb", tt - 1))
            s_u.inc(q.dma_start(out=s_t[:],
                                in_=cc_out[tt * 128:(tt + 1) * 128, :]), 16)
            s_u.inc(q.dma_start(out=rstat[:, 0:1],
                                in_=cc2_out[tt * 128:(tt + 1) * 128, :]), 16)
            s_u.inc(q.dma_start(out=uo2_t[:],
                                in_=uown_d[tt * 128:(tt + 1) * 128, :]), 16)
            s_u.inc(q.dma_start(out=g_t[:],
                                in_=gsum_d[:, tt * 128:(tt + 1) * 128],
                                transpose=True), 16)
            mark(("finld", tt), s_u)

        @blk.scalar
        def _(sc, tt=tt):
            waitm(sc, ("finld", tt))
            sc.activation(rstat[:, 1:2], rstat[:, 0:1], AF.Ln,
                          scale=1.0 / D_INNER, bias=cst[:, 3:4])
            s_ac.inc(sc.activation(rstat[:, 1:2], rstat[:, 1:2], AF.Exp,
                                   scale=-0.5))
            mark(("finrstd", tt), s_ac)

        @blk.vector
        def _(v, tt=tt):
            waitm(v, ("finrstd", tt))
            v.tensor_scalar_mul(s_t[:], s_t[:], rstat[:, 1:2])
            v.tensor_mul(o_t[:], s_t[:], g_t[:])
            s_ve.inc(v.tensor_add(o_t[:], o_t[:], uo2_t[:]))
            mark(("fincv", tt), s_ve)

        @blk.sync
        def _(q, tt=tt):
            waitm(q, ("fincv", tt))
            s_st.inc(q.dma_start(out=out_d[tt * 128:(tt + 1) * 128, :],
                                 in_=o_t[:]), 16)
            mark(("fincomb", tt), s_st)

    if "drams" in dbg:
        @blk.sync
        def _(q):
            q.wait_ge(s_st.h, s_st.n)
            q.wait_ge(s_cc.h, s_cc.n)
            s_db.inc(q.dma_start(out=dbg_d["xn"][:], in_=xn_scr[:]), 16)
            s_db.inc(q.dma_start(out=dbg_d["ccin"][:], in_=cc_in[:]), 16)
            s_db.inc(q.dma_start(out=dbg_d["cc2in"][:], in_=cc2_in[:]), 16)
            s_db.inc(q.dma_start(out=dbg_d["gsum"][:], in_=gsum_d[:]), 16)
            s_db.inc(q.dma_start(out=dbg_d["gp0"][:], in_=gpart[0]), 16)
            s_db.inc(q.dma_start(out=dbg_d["ccout"][:], in_=cc_out[:]), 16)

    @blk.sync
    def _(q):
        q.wait_ge(s_st.h, s_st.n)
        if s_db.n:
            q.wait_ge(s_db.h, s_db.n)

    f3_es.close()
    es.close()
    nc.finalize()
    return nc


def kernel(**inputs):
    seq = int(np.asarray(inputs["L0_attn_u"]).shape[1])
    in_maps = host_prepare(inputs, seq)
    nc = build_program(seq=seq, dbg=DBG)
    res = run_bass_kernel_spmd(nc, in_maps, core_ids=list(range(NCORE)))
    out = np.concatenate([np.asarray(res.results[c]["out"])
                          for c in range(NCORE)], axis=0)
    if DBG:
        kernel.dbg = res.results
    return out.reshape(1, seq, D_MODEL)


# version 06
# speedup vs baseline: 1.0060x; 1.0060x over previous
import math
import numpy as np
from contextlib import ExitStack

import concourse.bass as bass
import concourse.mybir as mybir
from concourse.bass_utils import run_bass_kernel_spmd

F32 = mybir.dt.float32
BF16 = mybir.dt.bfloat16
I32 = mybir.dt.int32
AF = mybir.ActivationFunctionType
ALU = mybir.AluOpType
AX = mybir.AxisListType.X

NCORE = 8
D_MODEL = 2048
D_INNER = 4096
NHEADS = 64
HD = 64
DSTATE = 128
CHUNK = 64
NEXP = 8
HPC = NHEADS // NCORE
KD = D_MODEL // 128
EPS = 1e-5
BIGNEG = -1e30
PI = math.pi

NIP = 1360
G1 = (0, 512)
G2 = (512, 848)
G3 = (848, 1360)

SEQ = 8192
DBG = ()


def _tile_k_major(w_t):
    C = w_t.shape[1]
    return np.ascontiguousarray(
        w_t.reshape(KD, 128, C).transpose(1, 0, 2).reshape(128, KD * C))


def host_prepare(inputs, seq):
    TOKC = seq // NCORE
    u = np.asarray(inputs["L0_attn_u"], np.float32).reshape(seq, D_MODEL)
    W_in = np.asarray(inputs["L0_attn_W_in"], np.float32)
    A_log = np.asarray(inputs["L0_attn_A_log"], np.float32)
    Dv = np.asarray(inputs["L0_attn_D"], np.float32)
    dt_bias = np.asarray(inputs["L0_attn_dt_bias"], np.float32)
    B_nw = np.asarray(inputs["L0_attn_B_norm_w"], np.float32)
    C_nw = np.asarray(inputs["L0_attn_C_norm_w"], np.float32)
    out_nw = np.asarray(inputs["L0_attn_out_norm_w"], np.float32)
    W_out = np.asarray(inputs["L0_attn_W_out"], np.float32)
    gw = np.asarray(inputs["L0_ffn_gate_weight"], np.float32)
    gu = np.asarray(inputs["L0_ffn_gate_up_proj"], np.float32)
    dp = np.asarray(inputs["L0_ffn_down_proj"], np.float32)

    tri = np.triu(np.ones((CHUNK, CHUNK), np.float32))
    triblk = np.zeros((128, 128), np.float32)
    triblk[:64, :64] = tri
    triblk[64:, 64:] = tri
    hi = np.tril(np.ones((CHUNK, CHUNK), np.float32), -1)
    triblk_hi = np.zeros((128, 128), np.float32)
    triblk_hi[:64, :64] = hi
    triblk_hi[64:, 64:] = hi
    triu128 = np.triu(np.ones((128, 128), np.float32))
    ind2neg = np.zeros((2, 128), np.float32)
    ind2neg[0, :64] = -1.0
    ind2neg[1, 64:] = -1.0
    delta8 = np.zeros((8, 512), np.float32)
    for h in range(8):
        delta8[h, h * 64:(h + 1) * 64] = 1.0
    bad = (np.arange(64)[:, None] > np.arange(64)[None, :]).astype(np.float32)
    tribias = np.zeros((128, 512), np.float32)
    for h in range(8):
        tribias[:64, h * 64:(h + 1) * 64] = bad * BIGNEG
        tribias[64:, h * 64:(h + 1) * 64] = bad * BIGNEG
    chunkcol = np.zeros((128, 2), np.float32)
    chunkcol[:64, 0] = 1.0
    chunkcol[64:, 1] = 1.0
    deltacol = np.zeros((8, 8 * 128), np.float32)
    for e in range(8):
        deltacol[e, e * 128:(e + 1) * 128] = 1.0
    ident = np.eye(128, dtype=np.float32)

    guT = np.ascontiguousarray(
        gu.transpose(0, 2, 1).reshape(NEXP, KD, 128, 64, 128)
        .transpose(0, 3, 2, 1, 4).reshape(NEXP, 64, 128, KD * 128))
    dpT = np.ascontiguousarray(
        dp.transpose(0, 2, 1).reshape(NEXP, 32, 128, 16, 128)
        .transpose(0, 3, 2, 1, 4).reshape(NEXP, 16, 128, 32 * 128))
    gwT = _tile_k_major(np.ascontiguousarray(gw.T))

    in_maps = []
    for core in range(NCORE):
        h0 = core * HPC
        rows = np.concatenate([
            W_in[D_INNER + h0 * HD:D_INNER + (h0 + HPC) * HD],
            W_in[2 * D_INNER:2 * D_INNER + DSTATE],
            W_in[2 * D_INNER + DSTATE:2 * D_INNER + 2 * DSTATE],
            W_in[2 * D_INNER + 2 * DSTATE + h0:
                 2 * D_INNER + 2 * DSTATE + h0 + HPC],
            W_in[2 * D_INNER + 2 * DSTATE + NHEADS + h0:
                 2 * D_INNER + 2 * DSTATE + NHEADS + h0 + HPC],
            W_in[2 * D_INNER + 2 * DSTATE + 2 * NHEADS:],
            W_in[h0 * HD:(h0 + HPC) * HD],
        ], axis=0)
        in_maps.append({
            "u": np.ascontiguousarray(u),
            "u_own": np.ascontiguousarray(u[core * TOKC:(core + 1) * TOKC]),
            "w_inT": _tile_k_major(np.ascontiguousarray(rows.T)),
            "w_outT": np.ascontiguousarray(
                np.ascontiguousarray(W_out[:, h0 * HD:(h0 + HPC) * HD].T)
                .reshape(4, 128, D_MODEL).transpose(1, 0, 2)
                .reshape(128, 4 * D_MODEL)),
            "gwT": gwT, "guT": guT, "dpT": dpT,
            "outnw": np.ascontiguousarray(
                out_nw[h0 * HD:(h0 + HPC) * HD].reshape(4, 128).T),
            "dtb_row": np.ascontiguousarray(dt_bias[None, h0:h0 + HPC]),
            "alog_row": np.ascontiguousarray(A_log[None, h0:h0 + HPC]),
            "bnw_row": np.ascontiguousarray(B_nw[None, :]),
            "cnw_row": np.ascontiguousarray(C_nw[None, :]),
            "d_hp": np.ascontiguousarray(
                np.repeat(Dv[h0:h0 + HPC], HD)[None, :]),
            "ident": ident, "triblk": triblk, "triblk_hi": triblk_hi,
            "triu128": triu128, "ind2neg": ind2neg, "delta8": delta8,
            "tribias": tribias, "chunkcol": chunkcol, "deltacol": deltacol,
        })
    return in_maps


def build_program(seq=SEQ, dbg=(), sim_cc_bypass=False, phases="full"):
    S = seq
    TOKC = S // NCORE
    NT = S // 128
    NB = S // 512
    NTO = TOKC // 128
    HALVES = [(i * 512, min(512, TOKC - i * 512))
              for i in range((TOKC + 511) // 512)]
    NHALF = len(HALVES)

    nc = bass.Bass("TRN2", target_bir_lowering=False,
                   detect_race_conditions=False)

    def din(name, shape, dtype=F32):
        return nc.dram_tensor(name, list(shape), dtype,
                              kind="ExternalInput").ap()

    u_d = din("u", (S, D_MODEL))
    uown_d = din("u_own", (TOKC, D_MODEL))
    w_inT_d = din("w_inT", (128, KD * NIP))
    w_outT_d = din("w_outT", (128, 4 * D_MODEL))
    gwT_d = din("gwT", (128, KD * 8))
    guT_d = din("guT", (NEXP, 64, 128, KD * 128))
    dpT_d = din("dpT", (NEXP, 16, 128, 32 * 128))
    outnw_d = din("outnw", (128, 4))
    dtb_d = din("dtb_row", (1, HPC))
    alog_d = din("alog_row", (1, HPC))
    bnw_d = din("bnw_row", (1, DSTATE))
    cnw_d = din("cnw_row", (1, DSTATE))
    dhp_d = din("d_hp", (1, 512))
    ident_d = din("ident", (128, 128))
    triblk_d = din("triblk", (128, 128))
    triblk_hi_d = din("triblk_hi", (128, 128))
    triu128_d = din("triu128", (128, 128))
    ind2neg_d = din("ind2neg", (2, 128))
    delta8_d = din("delta8", (8, 512))
    tribias_d = din("tribias", (128, 512))
    chunkcol_d = din("chunkcol", (128, 2))
    deltacol_d = din("deltacol", (8, 8 * 128))

    out_d = nc.dram_tensor("out", [TOKC, D_MODEL], F32,
                           kind="ExternalOutput").ap()

    xn_scr = nc.dram_tensor("xn_scr", [S, D_MODEL], BF16, kind="Internal").ap()
    xno_scr = nc.dram_tensor("xno_scr", [TOKC, D_MODEL], BF16,
                             kind="Internal").ap()
    flat_scr = nc.dram_tensor("flat_scr", [2, 8, 128], F32,
                              kind="Internal").ap()
    cc_in = nc.dram_tensor("cc_in", [S, D_MODEL], F32, kind="Internal").ap()
    cc_out = nc.dram_tensor("cc_out", [TOKC, D_MODEL], F32,
                            kind="Internal").ap()
    cc2_in = nc.dram_tensor("cc2_in", [S, 1], F32, kind="Internal").ap()
    cc2_out = nc.dram_tensor("cc2_out", [TOKC, 1], F32, kind="Internal").ap()
    gpart = nc.dram_tensor("gpart", [NEXP, D_MODEL, TOKC], F32,
                           kind="Internal").ap()
    gsum_d = nc.dram_tensor("gsum", [D_MODEL, TOKC], BF16, kind="Internal").ap()

    dbg_d = {}
    if "yg" in dbg:
        dbg_d["yg"] = nc.dram_tensor("dbg_yg", [S, 512], BF16,
                                     kind="ExternalOutput").ap()
    if "wexp" in dbg:
        dbg_d["wexp"] = nc.dram_tensor("dbg_wexp", [TOKC, 8], F32,
                                       kind="ExternalOutput").ap()
    if "y" in dbg:
        dbg_d["y"] = nc.dram_tensor("dbg_y", [S, 512], F32,
                                    kind="ExternalOutput").ap()
    if "drams" in dbg:
        dbg_d["xn"] = nc.dram_tensor("dbg_xn", [S, D_MODEL], BF16,
                                     kind="ExternalOutput").ap()
        dbg_d["ccin"] = nc.dram_tensor("dbg_ccin", [S, D_MODEL], F32,
                                       kind="ExternalOutput").ap()
        dbg_d["cc2in"] = nc.dram_tensor("dbg_cc2in", [S, 1], F32,
                                        kind="ExternalOutput").ap()
        dbg_d["gsum"] = nc.dram_tensor("dbg_gsum", [D_MODEL, TOKC], BF16,
                                       kind="ExternalOutput").ap()
        dbg_d["gp0"] = nc.dram_tensor("dbg_gp0", [D_MODEL, TOKC], F32,
                                      kind="ExternalOutput").ap()
        dbg_d["ccout"] = nc.dram_tensor("dbg_ccout", [TOKC, D_MODEL], F32,
                                        kind="ExternalOutput").ap()

    es = ExitStack()

    def sb(name, p, f, dtype, st=es):
        return st.enter_context(nc.sbuf_tensor(name, [p, f], dtype))

    class Sem:
        def __init__(self, name):
            self.h = es.enter_context(nc.semaphore(name))
            self.n = 0

        def inc(self, instr, k=1):
            instr.then_inc(self.h, k)
            self.n += k
            return self.n

    s_u = Sem("s_u")
    s_xt = Sem("s_xt")
    s_st = Sem("s_st")
    s_gw = Sem("s_gw")
    s_sh = Sem("s_sh")
    s_pe = Sem("s_pe")
    s_ve = Sem("s_ve")
    s_ac = Sem("s_ac")
    s_cc = Sem("s_cc")
    s_db = Sem("s_db")
    s_bar = Sem("s_bar")

    ident_bf = sb("ident_bf", 128, 128, BF16)
    ident_f = sb("ident_f", 128, 128, F32)
    ones1_bf = sb("ones1_bf", 1, 128, BF16)
    ones1_f = sb("ones1_f", 1, 128, F32)
    onescol_f = sb("onescol_f", 128, 1, F32)
    cst = sb("cst", 128, 4, F32)
    deltacol_bf = sb("deltacol_bf", 8, 8 * 128, BF16)
    gwT_s = sb("gwT_s", 128, KD * 8, BF16)
    vjunk = sb("vjunk", 128, 520, F32)
    junk = sb("junk", 128, D_MODEL, F32)

    PS = [es.enter_context(nc.psum_tensor("ps%d" % i, [128, 512], F32))
          for i in range(8)]
    B_YD, B_YO, B_MISC, B_LD, B_S1, B_S2, B_DEC, B_PRJ = range(8)
    bank_cons = [[] for _ in range(8)]

    def bank_touch(bank, sem):
        bank_cons[bank] = [(sem, sem.n)]

    def bank_wait(eng, bank):
        for sem, n in bank_cons[bank]:
            eng.wait_ge(sem.h, n)

    marks = {}

    def mark(key, sem):
        marks[key] = (sem, sem.n)

    def waitm(eng, key):
        if key in marks:
            sem, n = marks[key]
            eng.wait_ge(sem.h, n)

    blk = es.enter_context(nc.Block("prog"))

    def phase_barrier():
        target = s_bar.n + 5
        for sec in (blk.tensor, blk.vector, blk.scalar, blk.gpsimd, blk.sync):
            @sec
            def _(e):
                s_bar.inc(e.drain())
        for sec in (blk.tensor, blk.vector, blk.scalar, blk.gpsimd, blk.sync):
            @sec
            def _(e):
                e.wait_ge(s_bar.h, target)


    attn_es = ExitStack()

    def sba(name, p, f, dtype):
        return sb(name, p, f, dtype, attn_es)

    triblk_f = sba("triblk_f", 128, 128, F32)
    triblk_hi_f = sba("triblk_hi_f", 128, 128, F32)
    triu128_f = sba("triu128_f", 128, 128, F32)
    ind2neg_f = sba("ind2neg_f", 2, 128, F32)
    delta8_f = sba("delta8_f", 8, 512, F32)
    tribias_f = sba("tribias_f", 128, 512, F32)
    chunkcol_f = sba("chunkcol_f", 128, 2, F32)
    dtb_b = sba("dtb_b", 128, HPC, F32)
    ealog_b = sba("ealog_b", 128, HPC, F32)
    bnw_b = sba("bnw_b", 128, DSTATE, F32)
    cnw_b = sba("cnw_b", 128, DSTATE, F32)
    d_b = sba("d_b", 128, 512, F32)
    row_tmp = sba("row_tmp", 1, 512, F32)
    w_inT = sba("w_inT_s", 128, KD * NIP, BF16)
    w_outT = sba("w_outT_s", 128, 4 * D_MODEL, BF16)
    outnw_s = sba("outnw_s", 128, 4, F32)
    u_t = [sba("u_t0", 128, D_MODEL, F32)]
    xnst = [sba("xnst0", 128, D_MODEL, BF16)]
    statv = sba("statv", 128, 8, F32)
    stata = sba("stata", 128, 8, F32)
    xnT = [sba("xnT%d" % i, 128, KD * 512, BF16) for i in range(2)]
    xh_bf = sba("xh_bf", 128, 4 * 512, BF16)
    xd1_bf = sba("xd1_bf", 128, 4 * 512, BF16)
    xd2r_bf = sba("xd2r_bf", 128, 4 * 512, BF16)
    xd2_bf = sba("xd2_bf", 128, 4 * 512, BF16)
    xds_bf = sba("xds_bf", 128, 2 * 512, BF16)
    zs_bf = sba("zs_bf", 128, 4 * 512, BF16)
    braw = sba("braw", 128, 4 * 128, F32)
    craw = sba("craw", 128, 4 * 128, F32)
    bn_bf = sba("bn_bf", 128, 4 * 128, BF16)
    cn_bf = sba("cn_bf", 128, 4 * 128, BF16)
    b2_bf = sba("b2_bf", 128, 4 * 128, BF16)
    bprev = [sba("bprev%d" % i, 1, 128, BF16) for i in range(2)]
    xprev = [sba("xprev%d" % i, 1, 512, BF16) for i in range(2)]
    b_fm = sba("b_fm", 128, 4 * 128 + 1, BF16)
    c_fm = sba("c_fm", 128, 4 * 128, BF16)
    theta_f = sba("theta_f", 128, 4 * 64, F32)
    ang_f = sba("ang_f", 128, 256, F32)
    redk = sba("redk", 128, 256, I32)
    trig = sba("trig", 128, 512, F32)
    off_row = sba("off_row", 1, 64, F32)
    dtpath = sba("dtpath", 128, 4 * 48, F32)
    a_f = sba("a_f", 128, 4 * 8, F32)
    acs_f = sba("acs_f", 128, 4 * 8, F32)
    nacst_f = sba("nacst_f", 8, 4 * 128, F32)
    nflat = [sba("nflat%d" % i, 2, 512, F32) for i in range(2)]
    dstate_f = sba("dstate_f", 128, 4 * 8, F32)
    eacs_f = sba("eacs_f", 128, 4 * 8, F32)
    eatot_row = sba("eatot_row", 1, 4 * 16, F32)
    decrow_bf = sba("decrow_bf", 1, 512, BF16)
    ld_bf = [sba("ld_bf%d" % i, 128, 512, BF16) for i in range(2)]
    mw_bf = sba("mw_bf", 128, 2 * 512, BF16)
    inter = [sba("inter%d" % i, 128, 512, F32) for i in range(2)]
    inter_bf = [sba("interb%d" % i, 128, 512, BF16) for i in range(2)]
    sctmp = sba("sctmp", 128, 512, F32)
    y_f = sba("y_f", 128, 512, F32)
    ytmp = sba("ytmp", 128, 512, F32)
    yg_bf = sba("yg_bf", 128, 4 * 512, BF16)
    y_fm = sba("y_fm", 128, 4 * 512, BF16)
    ssq = sba("ssq", 128, NT, F32)
    stage = [sba("stage0", 128, D_MODEL, F32)]

    @blk.sync
    def _(q):
        for ap_s, ap_d in [(triu128_f[:], triu128_d), (ident_f[:], ident_d),
                           (triblk_f[:], triblk_d),
                           (triblk_hi_f[:], triblk_hi_d),
                           (ind2neg_f[:], ind2neg_d), (delta8_f[:], delta8_d),
                           (chunkcol_f[:], chunkcol_d),
                           (tribias_f[:], tribias_d), (dtb_b[0:1, :], dtb_d),
                           (bnw_b[0:1, :], bnw_d), (cnw_b[0:1, :], cnw_d),
                           (ealog_b[0:1, :], alog_d),
                           (row_tmp[0:1, 0:512], dhp_d), (outnw_s[:], outnw_d)]:
            s_u.inc(q.dma_start(out=ap_s, in_=ap_d), 16)
    mark("setup_u", s_u)

    @blk.gpsimd
    def _(q):
        for ap_s, ap_d in [(ident_bf[:], ident_d),
                           (deltacol_bf[:], deltacol_d),
                           (w_inT[:], w_inT_d), (w_outT[:], w_outT_d),
                           (gwT_s[:], gwT_d)]:
            s_gw.inc(q.dma_start(out=ap_s, in_=ap_d), 16)
    mark("setup_gw", s_gw)

    @blk.vector
    def _(v):
        v.memset(cst[:, 0:1], -PI)
        v.memset(cst[:, 1:2], PI / 2)
        v.memset(cst[:, 2:3], 1.0)
        v.memset(cst[:, 3:4], EPS)
        v.memset(ones1_bf[:], 1.0)
        v.memset(ones1_f[:], 1.0)
        v.memset(onescol_f[:], 1.0)
        v.memset(off_row[:], 0.0)
        v.memset(bprev[0][:], 0.0)
        v.memset(bprev[1][:], 0.0)
        v.memset(xprev[0][:], 0.0)
        v.memset(xprev[1][:], 0.0)
        v.memset(inter[0][:], 0.0)
        v.memset(inter[1][:], 0.0)
        v.memset(inter_bf[0][:], 0.0)
        v.memset(inter_bf[1][:], 0.0)
        v.memset(b_fm[:, 0:1], 0.0)
        v.wait_ge(s_u.h, marks["setup_u"][1])
        s_ve.inc(v.engine_nop())
    mark("setup_ve", s_ve)

    @blk.tensor
    def _(t):
        t.wait_ge(s_ve.h, marks["setup_ve"][1])
        t.matmul(PS[B_MISC][:, 0:HPC], ones1_f[:], dtb_b[0:1, :],
                 start=True, stop=True)
        t.matmul(PS[B_MISC][:, 8:8 + HPC], ones1_f[:], ealog_b[0:1, :],
                 start=True, stop=True)
        t.matmul(PS[B_LD][:, 0:DSTATE], ones1_f[:], bnw_b[0:1, :],
                 start=True, stop=True)
        t.matmul(PS[B_LD][:, 128:128 + DSTATE], ones1_f[:], cnw_b[0:1, :],
                 start=True, stop=True)
        s_pe.inc(t.matmul(PS[B_YD][:, 0:512], ones1_f[:], row_tmp[0:1, 0:512],
                          start=True, stop=True))
    mark("setup_pe", s_pe)

    @blk.vector
    def _(v):
        v.wait_ge(s_pe.h, marks["setup_pe"][1])
        v.tensor_copy(dtb_b[:], PS[B_MISC][:, 0:HPC])
        v.tensor_copy(ealog_b[:], PS[B_MISC][:, 8:8 + HPC])
        v.tensor_copy(bnw_b[:], PS[B_LD][:, 0:DSTATE])
        v.tensor_copy(cnw_b[:], PS[B_LD][:, 128:128 + DSTATE])
        s_ve.inc(v.tensor_copy(d_b[:], PS[B_YD][:, 0:512]))
    mark("setup_bc", s_ve)
    for _b in (B_MISC, B_LD, B_YD):
        bank_touch(_b, s_ve)

    @blk.scalar
    def _(sc):
        sc.wait_ge(s_ve.h, marks["setup_bc"][1])
        s_ac.inc(sc.activation(ealog_b[:], ealog_b[:], AF.Exp))
    mark("setup_ac", s_ac)

    @blk.vector
    def _(v):
        v.wait_ge(s_gw.h, marks["setup_gw"][1])
        v.wait_ge(s_ac.h, marks["setup_ac"][1])
        for q in range(4):
            i = v.tensor_scalar_mul(w_outT[:, q * D_MODEL:(q + 1) * D_MODEL],
                                    w_outT[:, q * D_MODEL:(q + 1) * D_MODEL],
                                    outnw_s[:, q:q + 1])
        s_ve.inc(i)
    mark("setup_wout", s_ve)

    def emit_ln(b):
        for j in range(4):
            tl = b * 4 + j

            @blk.sync
            def _(q, tl=tl):
                waitm(q, ("lnve", tl - 1))
                if tl == 16:
                    waitm(q, ("olnve", NTO - 1))
                s_u.inc(q.dma_start(out=u_t[0][:],
                                    in_=u_d[tl * 128:(tl + 1) * 128, :]), 16)
                mark(("lnu", tl), s_u)

            @blk.scalar
            def _(sc, tl=tl):
                waitm(sc, ("lnu", tl))
                waitm(sc, ("lnvar", tl - 1))
                s_ac.inc(sc.activation(junk[:], u_t[0][:], AF.Square,
                                       accum_out=stata[:, 0:1]))
                mark(("lnsq", tl), s_ac)

            @blk.vector
            def _(v, tl=tl):
                waitm(v, ("lnu", tl))
                v.reduce_sum(statv[:, 0:1], u_t[0][:], axis=AX)
                waitm(v, ("lnsq", tl))
                v.tensor_scalar_mul(statv[:, 3:4], stata[:, 0:1], 1.0 / D_MODEL)
                v.tensor_scalar_mul(statv[:, 2:3], statv[:, 0:1], 1.0 / D_MODEL)
                v.tensor_scalar_add(statv[:, 3:4], statv[:, 3:4], EPS)
                v.tensor_mul(statv[:, 4:5], statv[:, 2:3], statv[:, 2:3])
                v.tensor_copy(vjunk[:, 512:513], cst[:, 2:3])
                s_ve.inc(v.tensor_sub(statv[:, 5:6], statv[:, 3:4],
                                      statv[:, 4:5]))
                mark(("lnvar", tl), s_ve)

            @blk.scalar
            def _(sc, tl=tl):
                waitm(sc, ("lnvar", tl))
                sc.activation(stata[:, 1:2], statv[:, 5:6], AF.Ln)
                s_ac.inc(sc.activation(stata[:, 1:2], stata[:, 1:2], AF.Exp,
                                       scale=-0.5))
                mark(("lnrstd", tl), s_ac)

            @blk.vector
            def _(v, tl=tl):
                waitm(v, ("lnrstd", tl))
                waitm(v, ("lnst", tl - 1))
                v.tensor_scalar(xnst[0][:], u_t[0][:], statv[:, 2:3],
                                None, ALU.subtract)
                s_ve.inc(v.tensor_scalar_mul(xnst[0][:], xnst[0][:],
                                             stata[:, 1:2]))
                mark(("lnve", tl), s_ve)

            @blk.sync
            def _(q, tl=tl):
                waitm(q, ("lnve", tl))
                s_st.inc(q.dma_start(out=xn_scr[tl * 128:(tl + 1) * 128, :],
                                     in_=xnst[0][:]), 16)
                mark(("lnst", tl), s_st)
        mark(("lnblk", b), s_st)

    def emit_xt_read(b):
        @blk.sync
        def _(q):
            waitm(q, ("lnblk", b))
            waitm(q, ("ipdone", b - 2))
            for k in range(KD):
                i = q.dma_start(
                    out=xnT[b % 2][:, k * 512:(k + 1) * 512],
                    in_=xn_scr[b * 512:(b + 1) * 512, k * 128:(k + 1) * 128],
                    transpose=True)
                s_xt.inc(i, 16)
            mark(("xt", b), s_xt)

    def emit_attn_block(b):
        xb = xnT[b % 2]

        if b > 0:
            @blk.vector
            def _(v):
                waitm(v, ("scores", b * 4 - 1))
                s_ve.inc(v.tensor_copy(b_fm[:, 0:1], b_fm[:, 512:513]))
                mark(("bfmbnd", b), s_ve)

        for j in range(4):
            for gi, (g0, g1) in enumerate((G1, G2, G3)):
                gidx = j * 3 + gi
                bank = B_MISC if gidx % 2 == 0 else B_LD
                w = g1 - g0

                @blk.tensor
                def _(t, j=j, gi=gi, g0=g0, g1=g1, gidx=gidx, bank=bank, w=w):
                    if gidx == 0:
                        waitm(t, ("xt", b))
                    bank_wait(t, bank)
                    for k in range(KD):
                        lhs = xb[:, k * 512 + j * 128:k * 512 + (j + 1) * 128]
                        i = t.matmul(PS[bank][:, 0:w], lhs,
                                     w_inT[:, k * NIP + g0:k * NIP + g1],
                                     start=(k == 0), stop=(k == KD - 1))
                    s_pe.inc(i)
                    mark(("ipg", b, gidx), s_pe)

                if gi == 0:
                    @blk.vector
                    def _(v, j=j, bank=bank):
                        tl = b * 4 + j
                        waitm(v, ("ipg", b, j * 3))
                        waitm(v, ("yfmt", tl - 4))
                        s_ve.inc(v.tensor_copy(
                            xh_bf[:, j * 512:(j + 1) * 512],
                            PS[bank][:, 0:512]))
                        mark(("xhev", b, j), s_ve)
                        bank_touch(bank, s_ve)
                elif gi == 1:
                    @blk.vector
                    def _(v, j=j, bank=bank):
                        tl = b * 4 + j
                        waitm(v, ("ipg", b, j * 3 + 1))
                        waitm(v, ("apath", tl - 4))
                        waitm(v, ("acst", tl - 4))
                        v.tensor_copy(braw[:, j * 128:(j + 1) * 128],
                                      PS[bank][:, 0:128])
                        v.tensor_copy(craw[:, j * 128:(j + 1) * 128],
                                      PS[bank][:, 128:256])
                        o = j * 48
                        v.tensor_add(dtpath[:, o:o + 8], PS[bank][:, 256:264],
                                     dtb_b[:])
                        v.tensor_copy(dtpath[:, o + 40:o + 48],
                                      PS[bank][:, 264:272])
                        s_ve.inc(v.tensor_copy(theta_f[:, j * 64:(j + 1) * 64],
                                               PS[bank][:, 272:336]))
                        mark(("g2ev", b, j), s_ve)
                        bank_touch(bank, s_ve)
                else:
                    @blk.scalar
                    def _(sc, j=j, bank=bank):
                        waitm(sc, ("ipg", b, j * 3 + 2))
                        waitm(sc, ("ygate", b * 4 + j - 4))
                        s_ac.inc(sc.activation(
                            zs_bf[:, j * 512:(j + 1) * 512],
                            PS[bank][:, 0:512], AF.Silu))
                        mark(("zsev", b, j), s_ac)
                        bank_touch(bank, s_ac)

        marks[("ipdone", b)] = marks[("ipg", b, 11)]

        for j in range(4):
            @blk.scalar
            def _(sc, j=j):
                waitm(sc, ("g2ev", b, j))
                o = j * 48
                sc.activation(dtpath[:, o + 16:o + 24], dtpath[:, o:o + 8],
                              AF.Abs)
                sc.activation(dtpath[:, o + 16:o + 24],
                              dtpath[:, o + 16:o + 24], AF.Exp, scale=-1.0)
                sc.activation(dtpath[:, o + 16:o + 24],
                              dtpath[:, o + 16:o + 24], AF.Ln, bias=cst[:, 2:3])
                sc.activation(dtpath[:, o + 8:o + 16], dtpath[:, o:o + 8],
                              AF.Relu)
                s_ac.inc(sc.activation(dtpath[:, o + 32:o + 40],
                                       dtpath[:, o + 40:o + 48], AF.Exp,
                                       scale=-1.0))
                mark(("dtact", b, j), s_ac)

            @blk.vector
            def _(v, j=j):
                tl = b * 4 + j
                waitm(v, ("dtact", b, j))
                waitm(v, ("xhev", b, j))
                waitm(v, ("ssdmm", (tl - 4) * 2 + 1))
                o = j * 48
                v.tensor_scalar_add(dtpath[:, o + 32:o + 40],
                                    dtpath[:, o + 32:o + 40], 1.0)
                v.tensor_add(dtpath[:, o + 8:o + 16], dtpath[:, o + 8:o + 16],
                             dtpath[:, o + 16:o + 24])
                v.reciprocal(dtpath[:, o + 32:o + 40], dtpath[:, o + 32:o + 40])
                v.scalar_tensor_tensor(a_f[:, j * 8:(j + 1) * 8], ealog_b[:],
                                       -1.0, dtpath[:, o + 8:o + 16],
                                       ALU.mult, ALU.mult)
                v.tensor_mul(dtpath[:, o + 24:o + 32], dtpath[:, o + 8:o + 16],
                             dtpath[:, o + 32:o + 40])
                v.tensor_copy(vjunk[:, 512:513], cst[:, 2:3])
                v.tensor_sub(dtpath[:, o + 32:o + 40], dtpath[:, o + 8:o + 16],
                             dtpath[:, o + 24:o + 32])
                xh3 = xh_bf[:, j * 512:(j + 1) * 512].rearrange(
                    "p (h q) -> p h q", h=HPC)
                dl = dtpath[:, o + 24:o + 32]
                dm = dtpath[:, o + 32:o + 40]
                v.tensor_tensor(
                    xd1_bf[:, j * 512:(j + 1) * 512].rearrange(
                        "p (h q) -> p h q", h=HPC),
                    xh3, dl[:, :, None].broadcast_to([128, HPC, HD]), ALU.mult)
                s_ve.inc(v.tensor_tensor(
                    xd2r_bf[:, j * 512:(j + 1) * 512].rearrange(
                        "p (h q) -> p h q", h=HPC),
                    xh3, dm[:, :, None].broadcast_to([128, HPC, HD]), ALU.mult))
                mark(("dtve", b, j), s_ve)

        for j in range(4):
            tl = b * 4 + j

            @blk.tensor
            def _(t, j=j, tl=tl):
                waitm(t, ("dtve", b, j))
                waitm(t, ("apathev", tl - 1))
                bank_wait(t, B_MISC)
                ab = a_f[:, j * 8:(j + 1) * 8]
                t.matmul(PS[B_MISC][:, 128:136], triblk_f[:], ab,
                         start=True, stop=True)
                t.matmul(PS[B_MISC][:, 136:144], triblk_hi_f[:], ab,
                         start=True, stop=True)
                t.matmul(PS[B_MISC][0:1, 144 + j * 16:152 + j * 16],
                         chunkcol_f[:, 0:1], ab, start=True, stop=True)
                t.matmul(PS[B_MISC][0:1, 152 + j * 16:160 + j * 16],
                         chunkcol_f[:, 1:2], ab, start=True, stop=True)
                t.matmul(PS[B_MISC][:, 256:320], triu128_f[:],
                         theta_f[:, j * 64:(j + 1) * 64],
                         start=True, stop=False)
                t.matmul(PS[B_MISC][:, 256:320], ones1_f[:], off_row[:],
                         start=False, stop=True)
                s_pe.inc(t.matmul(PS[B_MISC][0:1, 320:384], onescol_f[:],
                                  theta_f[:, j * 64:(j + 1) * 64],
                                  start=True, stop=True))
                mark(("apath", tl), s_pe)

            @blk.vector
            def _(v, j=j, tl=tl):
                waitm(v, ("apath", tl))
                v.tensor_copy(acs_f[:, j * 8:(j + 1) * 8],
                              PS[B_MISC][:, 128:136])
                v.tensor_copy(ang_f[:, j * 64:(j + 1) * 64],
                              PS[B_MISC][:, 256:320])
                s_ve.inc(v.tensor_add(off_row[:], off_row[:],
                                      PS[B_MISC][0:1, 320:384]))
                mark(("apathev", tl), s_ve)
                bank_touch(B_MISC, s_ve)

            @blk.scalar
            def _(sc, j=j, tl=tl):
                waitm(sc, ("apath", tl))
                waitm(sc, ("mwb", tl - 4))
                waitm(sc, ("ygate", tl - 4))
                waitm(sc, ("decrow", tl - 3, 1))
                sc.activation(dstate_f[:, j * 8:(j + 1) * 8],
                              PS[B_MISC][:, 136:144], AF.Exp)
                sc.activation(eatot_row[0:1, j * 16:(j + 1) * 16],
                              PS[B_MISC][0:1, 144 + j * 16:160 + j * 16],
                              AF.Exp)
                waitm(sc, ("apathev", tl))
                s_ac.inc(sc.activation(eacs_f[:, j * 8:(j + 1) * 8],
                                       acs_f[:, j * 8:(j + 1) * 8], AF.Exp))
                mark(("aact", tl), s_ac)
                bank_touch(B_MISC, s_ac)

        @blk.vector
        def _(v):
            waitm(v, ("apathev", b * 4 + 3))
            v.tensor_scalar_mul(vjunk[:, 0:256], ang_f[:], 1.0 / (2 * PI))
            v.tensor_copy(redk[:], vjunk[:, 0:256])
            v.scalar_tensor_tensor(ang_f[:], redk[:], -2 * PI, ang_f[:],
                                   ALU.mult, ALU.add)
            v.tensor_scalar(vjunk[:, 0:256], ang_f[:], PI, None, ALU.is_gt)
            v.scalar_tensor_tensor(ang_f[:], vjunk[:, 0:256], -2 * PI,
                                   ang_f[:], ALU.mult, ALU.add)
            v.tensor_scalar(vjunk[:, 0:256], ang_f[:], -PI, None, ALU.is_lt)
            s_ve.inc(v.scalar_tensor_tensor(ang_f[:], vjunk[:, 0:256],
                                            2 * PI, ang_f[:],
                                            ALU.mult, ALU.add))
            mark(("angred", b), s_ve)

        @blk.scalar
        def _(sc):
            waitm(sc, ("angred", b))
            waitm(sc, ("rope", b * 4 - 1))
            sc.activation(trig[:, 0:256], ang_f[:], AF.Abs)
            sc.activation(trig[:, 0:256], trig[:, 0:256], AF.Sin,
                          bias=cst[:, 1:2], scale=-1.0)
            s_ac.inc(sc.activation(trig[:, 256:512], ang_f[:], AF.Sin))
            mark(("trig", b), s_ac)

        for j in range(4):
            tl = b * 4 + j

            @blk.tensor
            def _(t, j=j, tl=tl):
                waitm(t, ("aact", tl))
                bank_wait(t, B_PRJ)
                s_pe.inc(t.transpose(PS[B_PRJ][0:8, 0:128],
                                     acs_f[:, j * 8:(j + 1) * 8], ident_f[:]))
                mark(("acst", tl), s_pe)

            @blk.vector
            def _(v, j=j, tl=tl):
                waitm(v, ("acst", tl))
                waitm(v, ("ldmm", tl - 4))
                s_ve.inc(v.tensor_scalar_mul(
                    nacst_f[:, j * 128:(j + 1) * 128],
                    PS[B_PRJ][0:8, 0:128], -1.0))
                mark(("nacst", tl), s_ve)
                bank_touch(B_PRJ, s_ve)

            @blk.gpsimd
            def _(g, j=j, tl=tl):
                waitm(g, ("nacst", tl))
                waitm(g, ("ldmm", tl - 2))
                i = g.dma_start(out=flat_scr[tl % 2],
                                in_=nacst_f[:, j * 128:(j + 1) * 128])
                s_gw.inc(i, 16)
                g.wait_ge(s_gw.h, s_gw.n)
                s_gw.inc(g.dma_start(
                    out=nflat[tl % 2][:].rearrange("p (h s) -> p h s", h=8),
                    in_=flat_scr[tl % 2].rearrange("h (cc s) -> cc h s",
                                                   cc=2)), 16)
                mark(("nflat", tl), s_gw)

            @blk.scalar
            def _(sc, j=j, tl=tl):
                waitm(sc, ("g2ev", b, j))
                waitm(sc, ("rope", tl - 2))
                rb = 4 + (tl % 2) * 2
                sc.activation(junk[:, 0:128], braw[:, j * 128:(j + 1) * 128],
                              AF.Square, accum_out=stata[:, rb:rb + 1])
                sc.activation(junk[:, 128:256], craw[:, j * 128:(j + 1) * 128],
                              AF.Square, accum_out=stata[:, rb + 1:rb + 2])
                sc.activation(stata[:, rb:rb + 1], stata[:, rb:rb + 1], AF.Ln,
                              scale=1.0 / DSTATE, bias=cst[:, 3:4])
                sc.activation(stata[:, rb + 1:rb + 2], stata[:, rb + 1:rb + 2],
                              AF.Ln, scale=1.0 / DSTATE, bias=cst[:, 3:4])
                sc.activation(stata[:, rb:rb + 1], stata[:, rb:rb + 1], AF.Exp,
                              scale=-0.5)
                s_ac.inc(sc.activation(stata[:, rb + 1:rb + 2],
                                       stata[:, rb + 1:rb + 2], AF.Exp,
                                       scale=-0.5))
                mark(("bcrstd", tl), s_ac)

            @blk.vector
            def _(v, j=j, tl=tl):
                waitm(v, ("bcrstd", tl))
                waitm(v, ("trig", b))
                waitm(v, ("stmm", (tl - 4) * 2 + 1))
                waitm(v, ("shift", tl - 4))
                rb = 4 + (tl % 2) * 2
                cosj = trig[:, j * 64:(j + 1) * 64]
                sinj = trig[:, 256 + j * 64:256 + (j + 1) * 64]
                for (raw, nb, wt, rc) in ((braw, bn_bf, bnw_b, rb),
                                          (craw, cn_bf, cnw_b, rb + 1)):
                    sl = slice(j * 128, (j + 1) * 128)
                    v.tensor_mul(vjunk[:, 0:128], raw[:, sl], wt[:])
                    v.tensor_scalar_mul(vjunk[:, 0:128], vjunk[:, 0:128],
                                        stata[:, rc:rc + 1])
                    v0 = vjunk[:, 0:128:2]
                    v1 = vjunk[:, 1:128:2]
                    v.tensor_mul(vjunk[:, 256:320], cosj, v0)
                    v.tensor_mul(vjunk[:, 320:384], sinj, v1)
                    v.tensor_mul(vjunk[:, 384:448], sinj, v0)
                    v.tensor_mul(vjunk[:, 448:512], cosj, v1)
                    v.tensor_sub(nb[:, j * 128:(j + 1) * 128:2],
                                 vjunk[:, 256:320], vjunk[:, 320:384])
                    i = v.tensor_add(nb[:, j * 128 + 1:(j + 1) * 128:2],
                                     vjunk[:, 384:448], vjunk[:, 448:512])
                s_ve.inc(i)
                mark(("rope", tl), s_ve)

            @blk.scalar
            def _(sc, j=j, tl=tl):
                waitm(sc, ("rope", tl))
                waitm(sc, ("dtve", b, j))
                waitm(sc, ("stmm", (tl - 4) * 2 + 1))
                sl = slice(j * 128, (j + 1) * 128)
                s_sh.inc(sc.dma_start(out=b2_bf[0:1, sl],
                                      in_=bprev[tl % 2][:]), 16)
                s_sh.inc(sc.dma_start(out=b2_bf[1:128, sl],
                                      in_=bn_bf[0:127, sl]), 16)
                s_sh.inc(sc.dma_start(out=bprev[(tl + 1) % 2][:],
                                      in_=bn_bf[127:128, sl]), 16)
                sl2 = slice(j * 512, (j + 1) * 512)
                s_sh.inc(sc.dma_start(out=xd2_bf[0:1, sl2],
                                      in_=xprev[tl % 2][:]), 16)
                s_sh.inc(sc.dma_start(out=xd2_bf[1:128, sl2],
                                      in_=xd2r_bf[0:127, sl2]), 16)
                s_sh.inc(sc.dma_start(out=xprev[(tl + 1) % 2][:],
                                      in_=xd2r_bf[127:128, sl2]), 16)
                mark(("shift", tl), s_sh)

            @blk.tensor
            def _(t, j=j, tl=tl):
                waitm(t, ("rope", tl))
                bank_wait(t, B_PRJ)
                bp = PS[B_PRJ].bitcast(BF16)
                t.transpose(bp[:, 0:128], bn_bf[:, j * 128:(j + 1) * 128],
                            ident_bf[:])
                s_pe.inc(t.transpose(bp[:, 128:256],
                                     cn_bf[:, j * 128:(j + 1) * 128],
                                     ident_bf[:]))
                mark(("bcfm", tl), s_pe)

            @blk.vector
            def _(v, j=j, tl=tl):
                waitm(v, ("bcfm", tl))
                waitm(v, ("ssdmm", (tl - 4) * 2 + 1))
                bp = PS[B_PRJ].bitcast(BF16)
                v.tensor_copy(b_fm[:, 1 + j * 128:1 + (j + 1) * 128],
                              bp[:, 0:128])
                s_ve.inc(v.tensor_copy(c_fm[:, j * 128:(j + 1) * 128],
                                       bp[:, 128:256]))
                mark(("bcfmev", tl), s_ve)
                bank_touch(B_PRJ, s_ve)

            @blk.tensor
            def _(t, j=j, tl=tl):
                waitm(t, ("nflat", tl))
                bank_wait(t, B_LD)
                t.matmul(PS[B_LD][:, 0:512], ind2neg_f[:], nflat[tl % 2][:],
                         start=True, stop=False)
                s_pe.inc(t.matmul(PS[B_LD][:, 0:512],
                                  nacst_f[:, j * 128:(j + 1) * 128],
                                  delta8_f[:], start=False, stop=True))
                mark(("ldmm", tl), s_pe)

            @blk.vector
            def _(v, j=j, tl=tl):
                waitm(v, ("ldmm", tl))
                s_ve.inc(v.tensor_add(sctmp[:], PS[B_LD][:, 0:512],
                                      tribias_f[:]))
                mark(("ldadd", tl), s_ve)
                bank_touch(B_LD, s_ve)

            @blk.scalar
            def _(sc, j=j, tl=tl):
                waitm(sc, ("ldadd", tl))
                waitm(sc, ("mwb", tl - 2))
                s_ac.inc(sc.activation(ld_bf[tl % 2][:], sctmp[:], AF.Exp))
                mark(("ldiff", tl), s_ac)

            @blk.tensor
            def _(t, j=j, tl=tl):
                waitm(t, ("bcfmev", tl))
                waitm(t, ("shift", tl))
                bank_wait(t, B_MISC)
                for rho in range(2):
                    c0 = 1 + j * 128 + rho * 64
                    csl = slice(j * 128 + rho * 64, j * 128 + (rho + 1) * 64)
                    t.matmul(PS[B_MISC][rho * 64:(rho + 1) * 64, 0:64],
                             b_fm[:, c0:c0 + 64], c_fm[:, csl],
                             start=True, stop=True)
                    i = t.matmul(PS[B_MISC][rho * 64:(rho + 1) * 64, 64:128],
                                 b_fm[:, c0 - 1:c0 + 63], c_fm[:, csl],
                                 start=True, stop=True)
                s_pe.inc(i)
                mark(("scores", tl), s_pe)

            @blk.vector
            def _(v, j=j, tl=tl):
                waitm(v, ("scores", tl))
                waitm(v, ("ldiff", tl))
                waitm(v, ("ssdmm", tl * 2 - 1))
                for rho in range(2):
                    r = slice(rho * 64, (rho + 1) * 64)
                    for ssd in range(2):
                        sc_ap = PS[B_MISC][r, ssd * 64:(ssd + 1) * 64]
                        v.tensor_tensor(
                            mw_bf[r, ssd * 512:(ssd + 1) * 512].rearrange(
                                "p (h q) -> p h q", h=HPC),
                            sc_ap[:, None, :].broadcast_to([64, HPC, 64]),
                            ld_bf[tl % 2][r, :].rearrange(
                                "p (h q) -> p h q", h=HPC),
                            ALU.mult)
                ds3 = dstate_f[:, j * 8:(j + 1) * 8]
                v.tensor_tensor(
                    xds_bf[:, 0:512].rearrange("p (h q) -> p h q", h=HPC),
                    xd1_bf[:, j * 512:(j + 1) * 512].rearrange(
                        "p (h q) -> p h q", h=HPC),
                    ds3[:, :, None].broadcast_to([128, HPC, HD]), ALU.mult)
                s_ve.inc(v.tensor_tensor(
                    xds_bf[:, 512:1024].rearrange("p (h q) -> p h q", h=HPC),
                    xd2_bf[:, j * 512:(j + 1) * 512].rearrange(
                        "p (h q) -> p h q", h=HPC),
                    ds3[:, :, None].broadcast_to([128, HPC, HD]), ALU.mult))
                mark(("mwb", tl), s_ve)
                bank_touch(B_MISC, s_ve)

            @blk.vector
            def _(v, j=j, tl=tl):
                waitm(v, ("aact", tl))
                waitm(v, ("stmm", tl * 2 - 1))
                s_ve.inc(v.tensor_copy(
                    decrow_bf[0:1, :].rearrange("p (h q) -> p h q", h=HPC),
                    eatot_row[0:1, j * 16:j * 16 + 8]
                    [:, :, None].broadcast_to([1, HPC, HD])))
                mark(("decrow", tl, 0), s_ve)

            for rho in range(2):
                c = tl * 2 + rho

                @blk.tensor
                def _(t, j=j, tl=tl, rho=rho, c=c):
                    waitm(t, ("mwb", tl))
                    r = slice(rho * 64, (rho + 1) * 64)
                    bank_wait(t, B_S1)
                    bank_wait(t, B_S2)
                    t.matmul(PS[B_S1][:, :], bn_bf[r, j * 128:(j + 1) * 128],
                             xds_bf[r, 0:512], start=True, stop=True)
                    t.matmul(PS[B_S2][:, :], b2_bf[r, j * 128:(j + 1) * 128],
                             xds_bf[r, 512:1024], start=True, stop=True)
                    waitm(t, ("decrow", tl, rho))
                    bank_wait(t, B_DEC)
                    i = t.matmul(PS[B_DEC][:, :], ones1_bf[:],
                                 decrow_bf[0:1, :], start=True, stop=True)
                    s_pe.inc(i)
                    mark(("stmm", c), s_pe)
                    bank_wait(t, B_YD)
                    for h in range(HPC):
                        for ssd in range(2):
                            xs_ap = xd1_bf if ssd == 0 else xd2_bf
                            t.matmul(
                                PS[B_YD][r, h * 64:(h + 1) * 64],
                                mw_bf[r, ssd * 512 + h * 64:
                                      ssd * 512 + (h + 1) * 64],
                                xs_ap[r, j * 512 + h * 64:
                                      j * 512 + (h + 1) * 64],
                                start=(ssd == 0), stop=(ssd == 1))
                    waitm(t, ("interbf", c))
                    bank_wait(t, B_YO)
                    csl = slice(j * 128 + rho * 64, j * 128 + (rho + 1) * 64)
                    t.matmul(PS[B_YO][r, :], c_fm[:, csl], inter_bf[0][:],
                             start=True, stop=False)
                    i = t.matmul(PS[B_YO][r, :], c_fm[:, csl], inter_bf[1][:],
                                 start=False, stop=True)
                    s_pe.inc(i)
                    mark(("ssdmm", c), s_pe)

                @blk.vector
                def _(v, j=j, tl=tl, rho=rho, c=c):
                    waitm(v, ("stmm", c))
                    for ssd in range(2):
                        bank = PS[B_S1] if ssd == 0 else PS[B_S2]
                        v.tensor_tensor(sctmp[:], inter[ssd][:], PS[B_DEC][:],
                                        ALU.mult)
                        v.tensor_add(inter[ssd][:], sctmp[:], bank[:])
                    s_ve.inc(v.engine_nop())
                    mark(("scan", c), s_ve)
                    bank_touch(B_S1, s_ve)
                    bank_touch(B_S2, s_ve)
                    bank_touch(B_DEC, s_ve)
                    waitm(v, ("ssdmm", c))
                    v.tensor_copy(inter_bf[0][:], inter[0][:])
                    s_ve.inc(v.tensor_copy(inter_bf[1][:], inter[1][:]))
                    mark(("interbf", c + 1), s_ve)
                    if rho == 0:
                        s_ve.inc(v.tensor_copy(
                            decrow_bf[0:1, :].rearrange("p (h q) -> p h q",
                                                        h=HPC),
                            eatot_row[0:1, j * 16 + 8:j * 16 + 16]
                            [:, :, None].broadcast_to([1, HPC, HD])))
                        mark(("decrow", tl, 1), s_ve)

            @blk.vector
            def _(v, j=j, tl=tl):
                waitm(v, ("ssdmm", tl * 2 + 1))
                waitm(v, ("zsev", b, j))
                e3 = eacs_f[:, j * 8:(j + 1) * 8]
                v.tensor_tensor(
                    ytmp[:].rearrange("p (h q) -> p h q", h=HPC),
                    PS[B_YO][:].rearrange("p (h q) -> p h q", h=HPC),
                    e3[:, :, None].broadcast_to([128, HPC, HD]), ALU.mult)
                v.tensor_add(y_f[:], PS[B_YD][:], ytmp[:])
                bank_touch(B_YD, s_ve)
                bank_touch(B_YO, s_ve)
                v.tensor_mul(ytmp[:], xh_bf[:, j * 512:(j + 1) * 512], d_b[:])
                v.tensor_add(y_f[:], y_f[:], ytmp[:])
                s_ve.inc(v.tensor_mul(yg_bf[:, j * 512:(j + 1) * 512], y_f[:],
                                      zs_bf[:, j * 512:(j + 1) * 512]))
                mark(("ygate", tl), s_ve)

            if "yg" in dbg_d or "y" in dbg_d:
                @blk.sync
                def _(q, j=j, tl=tl):
                    waitm(q, ("ygate", tl))
                    if "y" in dbg_d:
                        s_db.inc(q.dma_start(
                            out=dbg_d["y"][tl * 128:(tl + 1) * 128, :],
                            in_=y_f[:]), 16)
                    if "yg" in dbg_d:
                        s_db.inc(q.dma_start(
                            out=dbg_d["yg"][tl * 128:(tl + 1) * 128, :],
                            in_=yg_bf[:, j * 512:(j + 1) * 512]), 16)
                    mark(("dbgyg", tl), s_db)

            @blk.scalar
            def _(sc, j=j, tl=tl):
                waitm(sc, ("ygate", tl))
                s_ac.inc(sc.activation(junk[:, 0:512],
                                       yg_bf[:, j * 512:(j + 1) * 512],
                                       AF.Square, accum_out=ssq[:, tl:tl + 1]))
                mark(("ssqt", tl), s_ac)

            @blk.tensor
            def _(t, j=j, tl=tl):
                waitm(t, ("ygate", tl))
                if ("dbgyg", tl) in marks:
                    waitm(t, ("dbgyg", tl))
                bank_wait(t, B_PRJ)
                bp = PS[B_PRJ].bitcast(BF16)
                for q in range(4):
                    i = t.transpose(
                        bp[:, q * 128:(q + 1) * 128],
                        yg_bf[:, j * 512 + q * 128:j * 512 + (q + 1) * 128],
                        ident_bf[:])
                s_pe.inc(i)
                mark(("yfmt", tl), s_pe)

            @blk.vector
            def _(v, j=j, tl=tl):
                waitm(v, ("yfmt", tl))
                waitm(v, ("opmm", tl - 4, 3))
                bp = PS[B_PRJ].bitcast(BF16)
                for q in range(4):
                    i = v.tensor_copy(y_fm[:, q * 512 + j * 128:
                                           q * 512 + (j + 1) * 128],
                                      bp[:, q * 128:(q + 1) * 128])
                s_ve.inc(i)
                mark(("yfm", tl), s_ve)
                bank_touch(B_PRJ, s_ve)

        for j in range(4):
            tl = b * 4 + j

            for g in range(4):
                bank = B_DEC if g % 2 == 0 else B_PRJ

                @blk.tensor
                def _(t, j=j, tl=tl, g=g, bank=bank):
                    if g == 0:
                        waitm(t, ("yfm", b * 4 + 3))
                    bank_wait(t, bank)
                    for k in range(4):
                        i = t.matmul(
                            PS[bank][:, :],
                            y_fm[:, k * 512 + j * 128:k * 512 + (j + 1) * 128],
                            w_outT[:, k * D_MODEL + g * 512:
                                   k * D_MODEL + (g + 1) * 512],
                            start=(k == 0), stop=(k == 3))
                    s_pe.inc(i)
                    mark(("opmm", tl, g), s_pe)

                @blk.vector
                def _(v, g=g, bank=bank, tl=tl):
                    waitm(v, ("opmm", tl, g))
                    if g == 0:
                        waitm(v, ("stgdma", tl - 1))
                    i = v.tensor_copy(
                        stage[0][:, g * 512:(g + 1) * 512],
                        PS[bank][:, :])
                    s_ve.inc(i)
                    mark(("opev", tl, g), s_ve)
                    bank_touch(bank, s_ve)

            @blk.sync
            def _(q, tl=tl):
                waitm(q, ("opev", tl, 3))
                s_st.inc(q.dma_start(out=cc_in[tl * 128:(tl + 1) * 128, :],
                                     in_=stage[0][:]), 16)
                mark(("stgdma", tl), s_st)

    def emit_oln():
        for tt in range(NTO):
            @blk.sync
            def _(q, tt=tt):
                waitm(q, ("olnve", tt - 1))
                if tt == 0:
                    waitm(q, ("lnve", min(4, NB) * 4 - 1))
                s_u.inc(q.dma_start(out=u_t[0][:],
                                    in_=uown_d[tt * 128:(tt + 1) * 128, :]), 16)
                mark(("olnu", tt), s_u)

            @blk.scalar
            def _(sc, tt=tt):
                waitm(sc, ("olnu", tt))
                waitm(sc, ("olnvar", tt - 1))
                s_ac.inc(sc.activation(junk[:], u_t[0][:], AF.Square,
                                       accum_out=stata[:, 0:1]))
                mark(("olnsq", tt), s_ac)

            @blk.vector
            def _(v, tt=tt):
                waitm(v, ("olnu", tt))
                v.reduce_sum(statv[:, 0:1], u_t[0][:], axis=AX)
                waitm(v, ("olnsq", tt))
                v.tensor_scalar_mul(statv[:, 3:4], stata[:, 0:1], 1.0 / D_MODEL)
                v.tensor_scalar_mul(statv[:, 2:3], statv[:, 0:1], 1.0 / D_MODEL)
                v.tensor_scalar_add(statv[:, 3:4], statv[:, 3:4], EPS)
                v.tensor_mul(statv[:, 4:5], statv[:, 2:3], statv[:, 2:3])
                v.tensor_copy(vjunk[:, 512:513], cst[:, 2:3])
                s_ve.inc(v.tensor_sub(statv[:, 5:6], statv[:, 3:4],
                                      statv[:, 4:5]))
                mark(("olnvar", tt), s_ve)

            @blk.scalar
            def _(sc, tt=tt):
                waitm(sc, ("olnvar", tt))
                sc.activation(stata[:, 1:2], statv[:, 5:6], AF.Ln)
                s_ac.inc(sc.activation(stata[:, 1:2], stata[:, 1:2], AF.Exp,
                                       scale=-0.5))
                mark(("olnrstd", tt), s_ac)

            @blk.vector
            def _(v, tt=tt):
                waitm(v, ("olnrstd", tt))
                waitm(v, ("olnst", tt - 1))
                v.tensor_scalar(xnst[0][:], u_t[0][:], statv[:, 2:3], None,
                                ALU.subtract)
                s_ve.inc(v.tensor_scalar_mul(xnst[0][:], xnst[0][:],
                                             stata[:, 1:2]))
                mark(("olnve", tt), s_ve)

            @blk.sync
            def _(q, tt=tt):
                waitm(q, ("olnve", tt))
                s_st.inc(q.dma_start(out=xno_scr[tt * 128:(tt + 1) * 128, :],
                                     in_=xnst[0][:]), 16)
                mark(("olnst", tt), s_st)

    emit_ln(0)
    if NB > 1:
        emit_ln(1)
    if NB > 2:
        emit_ln(2)
    if NB > 3:
        emit_ln(3)
    emit_oln()
    if phases != "ln":
        emit_xt_read(0)
    for b in range(NB):
        if b + 4 < NB:
            emit_ln(b + 4)
        if phases == "ln":
            continue
        if b + 1 < NB:
            emit_xt_read(b + 1)
        emit_attn_block(b)

    if phases == "ln":
        if "drams" in dbg:
            @blk.sync
            def _(q):
                q.wait_ge(s_st.h, s_st.n)
                s_db.inc(q.dma_start(out=dbg_d["xn"][:], in_=xn_scr[:]), 16)

        @blk.sync
        def _(q):
            q.wait_ge(s_st.h, s_st.n)
            if s_db.n:
                q.wait_ge(s_db.h, s_db.n)
        attn_es.close()
        es.close()
        nc.finalize()
        return nc

    @blk.tensor
    def _(t):
        waitm(t, ("ssqt", NT - 1))
        bank_wait(t, B_MISC)
        s_pe.inc(t.transpose(PS[B_MISC][0:min(NT, 128), 0:128],
                             ssq[:, 0:NT], ident_f[:]))
        mark("ssqT", s_pe)

    @blk.vector
    def _(v):
        waitm(v, "ssqT")
        s_ve.inc(v.tensor_copy(junk[0:NT, 0:128], PS[B_MISC][0:NT, 0:128]))
        mark("ssqTev", s_ve)
        bank_touch(B_MISC, s_ve)

    @blk.sync
    def _(q):
        waitm(q, "ssqTev")
        s_st.inc(q.dma_start(
            out=cc2_in.rearrange("(t p) one -> t (p one)", p=128),
            in_=junk[0:NT, 0:128]), 16)
        mark("cc2st", s_st)

    mark("ccready", s_st)

    def emit_collective():
        @blk.gpsimd
        def _(g):
            waitm(g, "ccready")
            if sim_cc_bypass:
                g.dma_start(out=cc_out[:],
                            in_=cc_in[0:TOKC, :]).then_inc(s_cc.h, 16)
                g.dma_start(out=cc2_out[:],
                            in_=cc2_in[0:TOKC, :]).then_inc(s_cc.h, 16)
                s_cc.n += 32
            else:
                i1 = g.collective_compute(
                    "ReduceScatter", ALU.add,
                    replica_groups=[list(range(NCORE))],
                    ins=[cc_in[:]], outs=[cc_out[:]])
                s_cc.inc(i1, 1)
                i2 = g.collective_compute(
                    "ReduceScatter", ALU.add,
                    replica_groups=[list(range(NCORE))],
                    ins=[cc2_in[:]], outs=[cc2_out[:]])
                s_cc.inc(i2, 1)
        mark("ccdone", s_cc)

    phase_barrier()
    attn_es.close()


    f2_es = ExitStack()
    xnT_own = sb("xnT_own", 128, KD * TOKC, BF16, f2_es)
    gu_s = [sb("gu_s%d" % i, 128, KD * 128, BF16, f2_es) for i in range(2)]
    dp_s = [sb("dp_s%d" % i, 128, 32 * 128, BF16, f2_es) for i in range(2)]
    h_fm = sb("h_fm", 128, 32 * TOKC, BF16, f2_es)
    gld = [sb("gld%d" % i, 128, TOKC, F32, f2_es) for i in range(2)]
    gacc = sb("gacc", 128, TOKC, F32, f2_es)
    gs_bf = sb("gs_bf", 128, TOKC, BF16, f2_es)
    silu_t = sb("silu_t", 128, TOKC, BF16, f2_es)
    probs_s = sb("probs_s", 128, NTO * 8, F32, f2_es)
    wexp_s = sb("wexp_s", 128, NTO * 8, F32, f2_es)
    wrow_bf = sb("wrow_bf", 8, NTO * 128, BF16, f2_es)
    wb_bf = sb("wb_bf", 128, TOKC, BF16, f2_es)
    f2stat = sb("f2stat", 128, 8 * max(8, 1), F32, f2_es)
    fstage = [sb("fstage%d" % i, 128, TOKC, F32, f2_es) for i in range(2)]

    @blk.sync
    def _(q):
        waitm(q, ("olnst", NTO - 1))
        for k in range(KD):
            i = q.dma_start(out=xnT_own[:, k * TOKC:(k + 1) * TOKC],
                            in_=xno_scr[:, k * 128:(k + 1) * 128],
                            transpose=True)
            s_xt.inc(i, 16)
        mark("xtown", s_xt)

    @blk.tensor
    def _(t):
        waitm(t, "xtown")
        bank_wait(t, B_MISC)
        for tt in range(NTO):
            for k in range(KD):
                i = t.matmul(PS[B_MISC][:, tt * 8:(tt + 1) * 8],
                             xnT_own[:, k * TOKC + tt * 128:
                                     k * TOKC + (tt + 1) * 128],
                             gwT_s[:, k * 8:(k + 1) * 8],
                             start=(k == 0), stop=(k == KD - 1))
        s_pe.inc(i)
        mark("router", s_pe)

    @blk.vector
    def _(v):
        waitm(v, "router")
        for tt in range(NTO):
            v.reduce_max(f2stat[:, tt * 8:tt * 8 + 1],
                         PS[B_MISC][:, tt * 8:(tt + 1) * 8], axis=AX)
        for tt in range(NTO):
            sl = slice(tt * 8, (tt + 1) * 8)
            v.tensor_scalar(probs_s[:, sl], PS[B_MISC][:, sl],
                            f2stat[:, tt * 8:tt * 8 + 1], None, ALU.subtract)
        s_ve.inc(v.engine_nop())
        mark("rsub", s_ve)
        bank_touch(B_MISC, s_ve)

    @blk.scalar
    def _(sc):
        waitm(sc, "rsub")
        s_ac.inc(sc.activation(probs_s[:], probs_s[:], AF.Exp))
        mark("rexp", s_ac)

    @blk.vector
    def _(v):
        waitm(v, "rexp")
        SL = [slice(tt * 8, (tt + 1) * 8) for tt in range(NTO)]
        c = lambda tt, k: f2stat[:, tt * 8 + k:tt * 8 + k + 1]
        for tt in range(NTO):
            v.reduce_sum(c(tt, 1), probs_s[:, SL[tt]], axis=AX)
        for tt in range(NTO):
            v.reciprocal(c(tt, 2), c(tt, 1))
        for tt in range(NTO):
            v.tensor_scalar_mul(probs_s[:, SL[tt]], probs_s[:, SL[tt]],
                                c(tt, 2))
        for tt in range(NTO):
            v.reduce_max(c(tt, 3), probs_s[:, SL[tt]], axis=AX)
        for tt in range(NTO):
            v.tensor_scalar(wexp_s[:, SL[tt]], probs_s[:, SL[tt]], c(tt, 3),
                            None, ALU.is_ge)
        for tt in range(NTO):
            v.scalar_tensor_tensor(vjunk[:, tt * 8:(tt + 1) * 8],
                                   wexp_s[:, SL[tt]], BIGNEG,
                                   probs_s[:, SL[tt]], ALU.mult, ALU.add)
        for tt in range(NTO):
            v.reduce_max(c(tt, 4), vjunk[:, tt * 8:(tt + 1) * 8], axis=AX)
        for tt in range(NTO):
            v.tensor_scalar(wexp_s[:, SL[tt]], probs_s[:, SL[tt]], c(tt, 4),
                            None, ALU.is_ge)
        for tt in range(NTO):
            v.tensor_add(c(tt, 5), c(tt, 3), c(tt, 4))
        for tt in range(NTO):
            v.reciprocal(c(tt, 6), c(tt, 5))
        for tt in range(NTO):
            v.tensor_mul(wexp_s[:, SL[tt]], wexp_s[:, SL[tt]],
                         probs_s[:, SL[tt]])
        for tt in range(NTO):
            v.tensor_scalar_mul(wexp_s[:, SL[tt]], wexp_s[:, SL[tt]],
                                c(tt, 6))
        s_ve.inc(v.engine_nop())
        mark("topk", s_ve)

    if "wexp" in dbg_d:
        @blk.sync
        def _(q):
            waitm(q, "topk")
            s_db.inc(q.dma_start(
                out=dbg_d["wexp"].rearrange("(t p) e -> p (t e)", p=128),
                in_=wexp_s[:]), 16)

    @blk.tensor
    def _(t):
        waitm(t, "topk")
        bank_wait(t, B_LD)
        bank_wait(t, B_MISC)
        for tt in range(NTO):
            bank = B_LD if tt < 4 else B_MISC
            i = t.transpose(PS[bank][0:8, (tt % 4) * 128:(tt % 4 + 1) * 128],
                            wexp_s[:, tt * 8:(tt + 1) * 8], ident_f[:])
        s_pe.inc(i)
        mark("wrowT", s_pe)

    @blk.vector
    def _(v):
        waitm(v, "wrowT")
        for tt in range(NTO):
            bank = B_LD if tt < 4 else B_MISC
            i = v.tensor_copy(wrow_bf[:, tt * 128:(tt + 1) * 128],
                              PS[bank][0:8, (tt % 4) * 128:(tt % 4 + 1) * 128])
        s_ve.inc(i)
        mark("wrow", s_ve)
        bank_touch(B_LD, s_ve)
        bank_touch(B_MISC, s_ve)

    for e in range(NEXP):
        if e == 1:
            emit_collective()
        for hi_, (h0_, hw_) in enumerate(HALVES):
            @blk.tensor
            def _(t, e=e, hi_=hi_, h0_=h0_, hw_=hw_):
                if hi_ == 0:
                    waitm(t, "wrow")
                    waitm(t, ("dpmm", e - 1, 15))
                bank_wait(t, B_DEC)
                i = t.matmul(PS[B_DEC][:, 0:hw_],
                             deltacol_bf[:, e * 128:(e + 1) * 128],
                             wrow_bf[:, h0_:h0_ + hw_],
                             start=True, stop=True)
                s_pe.inc(i)
                mark(("wbmm", e, hi_), s_pe)

            @blk.vector
            def _(v, e=e, hi_=hi_, h0_=h0_, hw_=hw_):
                waitm(v, ("wbmm", e, hi_))
                i = v.tensor_copy(wb_bf[:, h0_:h0_ + hw_],
                                  PS[B_DEC][:, 0:hw_])
                s_ve.inc(i)
                mark(("wbev", e, hi_), s_ve)
                bank_touch(B_DEC, s_ve)

        for m in range(32):
            for part, mm in ((0, m), (1, m + 32)):
                @blk.gpsimd
                def _(g, e=e, m=m, part=part, mm=mm):
                    waitm(g, ("gumm", e, m - 1, part))
                    i = g.dma_start(out=gu_s[part][:], in_=guT_d[e, mm])
                    s_gw.inc(i, 16)
                    mark(("guld", e, m, part), s_gw)

                @blk.tensor
                def _(t, e=e, m=m, part=part, mm=mm):
                    waitm(t, ("guld", e, m, part))
                    banks = (B_S1, B_YD) if part == 0 else (B_S2, B_YO)
                    for half in range(NHALF):
                        bank_wait(t, banks[half])
                    for k in range(KD):
                        for half, (h0_, hw_) in enumerate(HALVES):
                            i = t.matmul(
                                PS[banks[half]][:, 0:hw_],
                                gu_s[part][:, k * 128:(k + 1) * 128],
                                xnT_own[:, k * TOKC + h0_:
                                        k * TOKC + h0_ + hw_],
                                start=(k == 0), stop=(k == KD - 1))
                    s_pe.inc(i)
                    mark(("gumm", e, m, part), s_pe)

            @blk.scalar
            def _(sc, e=e, m=m):
                waitm(sc, ("gumm", e, m, 0))
                waitm(sc, ("hev", e, m - 1))
                for half, (h0_, hw_) in enumerate(HALVES):
                    bank = PS[B_S1] if half == 0 else PS[B_YD]
                    i = sc.activation(silu_t[:, h0_:h0_ + hw_],
                                      bank[:, 0:hw_], AF.Silu)
                s_ac.inc(i)
                mark(("gsilu", e, m), s_ac)
                bank_touch(B_S1, s_ac)
                if NHALF > 1:
                    bank_touch(B_YD, s_ac)

            @blk.vector
            def _(v, e=e, m=m):
                waitm(v, ("gsilu", e, m))
                waitm(v, ("gumm", e, m, 1))
                waitm(v, ("dpmm", e - 1, 15))
                for half, (h0_, hw_) in enumerate(HALVES):
                    bank = PS[B_S2] if half == 0 else PS[B_YO]
                    i = v.tensor_mul(
                        h_fm[:, m * TOKC + h0_:m * TOKC + h0_ + hw_],
                        silu_t[:, h0_:h0_ + hw_], bank[:, 0:hw_])
                s_ve.inc(i)
                mark(("hev", e, m), s_ve)
                bank_touch(B_S2, s_ve)
                if NHALF > 1:
                    bank_touch(B_YO, s_ve)

        for m in range(16):
            @blk.gpsimd
            def _(g, e=e, m=m):
                waitm(g, ("dpmm", e, m - 2))
                i = g.dma_start(out=dp_s[m % 2][:], in_=dpT_d[e, m])
                s_gw.inc(i, 16)
                mark(("dpld", e, m), s_gw)

            @blk.tensor
            def _(t, e=e, m=m):
                waitm(t, ("dpld", e, m))
                waitm(t, ("hev", e, 31))
                banks = (B_LD, B_S1) if m % 2 == 0 else (B_MISC, B_S2)
                for half in range(NHALF):
                    bank_wait(t, banks[half])
                for k in range(32):
                    for half, (h0_, hw_) in enumerate(HALVES):
                        i = t.matmul(
                            PS[banks[half]][:, 0:hw_],
                            dp_s[m % 2][:, k * 128:(k + 1) * 128],
                            h_fm[:, k * TOKC + h0_:k * TOKC + h0_ + hw_],
                            start=(k == 0), stop=(k == 31))
                s_pe.inc(i)
                mark(("dpmm", e, m), s_pe)

            @blk.vector
            def _(v, e=e, m=m):
                waitm(v, ("dpmm", e, m))
                waitm(v, ("wbev", e, NHALF - 1))
                waitm(v, ("fstdma", e, m - 2))
                banks = (B_LD, B_S1) if m % 2 == 0 else (B_MISC, B_S2)
                for half, (h0_, hw_) in enumerate(HALVES):
                    i = v.tensor_mul(
                        fstage[m % 2][:, h0_:h0_ + hw_],
                        PS[banks[half]][:, 0:hw_],
                        wb_bf[:, h0_:h0_ + hw_])
                s_ve.inc(i)
                mark(("dpev", e, m), s_ve)
                for half in range(NHALF):
                    bank_touch(banks[half], s_ve)

            @blk.sync
            def _(q, e=e, m=m):
                waitm(q, ("dpev", e, m))
                s_st.inc(q.dma_start(out=gpart[e, m * 128:(m + 1) * 128, :],
                                     in_=fstage[m % 2][:]), 16)
                mark(("fstdma", e, m), s_st)

    for m in range(16):
        for e in range(NEXP):
            @blk.sync
            def _(q, m=m, e=e):
                waitm(q, ("fstdma", NEXP - 1, m))
                waitm(q, ("gaccv", m, e - 2))
                s_u.inc(q.dma_start(out=gld[e % 2][:],
                                    in_=gpart[e, m * 128:(m + 1) * 128, :]),
                        16)
                mark(("gldd", m, e), s_u)

            @blk.vector
            def _(v, m=m, e=e):
                waitm(v, ("gldd", m, e))
                if e == 0:
                    i = v.tensor_copy(gacc[:], gld[0][:])
                else:
                    i = v.tensor_add(gacc[:], gacc[:], gld[e % 2][:])
                s_ve.inc(i)
                mark(("gaccv", m, e), s_ve)

        @blk.scalar
        def _(sc, m=m):
            waitm(sc, ("gaccv", m, NEXP - 1))
            waitm(sc, ("gsumst", m - 1))
            s_ac.inc(sc.activation(gs_bf[:], gacc[:], AF.Silu))
            mark(("gsilu2", m), s_ac)

        @blk.sync
        def _(q, m=m):
            waitm(q, ("gsilu2", m))
            s_st.inc(q.dma_start(out=gsum_d[m * 128:(m + 1) * 128, :],
                                 in_=gs_bf[:]), 16)
            mark(("gsumst", m), s_st)


    phase_barrier()
    f2_es.close()

    f3_es = ExitStack()
    s_t = sb("s_t", 128, D_MODEL, F32, f3_es)
    g_t = sb("g_t", 128, D_MODEL, BF16, f3_es)
    uo2_t = sb("uo2_t", 128, D_MODEL, F32, f3_es)
    o_t = sb("o_t", 128, D_MODEL, F32, f3_es)
    rstat = sb("rstat", 128, 4, F32, f3_es)

    for tt in range(NTO):
        @blk.sync
        def _(q, tt=tt):
            if tt == 0:
                q.wait_ge(s_cc.h, s_cc.n)
            waitm(q, ("gsumst", 15))
            waitm(q, ("fincomb", tt - 1))
            s_u.inc(q.dma_start(out=s_t[:],
                                in_=cc_out[tt * 128:(tt + 1) * 128, :]), 16)
            s_u.inc(q.dma_start(out=rstat[:, 0:1],
                                in_=cc2_out[tt * 128:(tt + 1) * 128, :]), 16)
            s_u.inc(q.dma_start(out=uo2_t[:],
                                in_=uown_d[tt * 128:(tt + 1) * 128, :]), 16)
            s_u.inc(q.dma_start(out=g_t[:],
                                in_=gsum_d[:, tt * 128:(tt + 1) * 128],
                                transpose=True), 16)
            mark(("finld", tt), s_u)

        @blk.scalar
        def _(sc, tt=tt):
            waitm(sc, ("finld", tt))
            sc.activation(rstat[:, 1:2], rstat[:, 0:1], AF.Ln,
                          scale=1.0 / D_INNER, bias=cst[:, 3:4])
            s_ac.inc(sc.activation(rstat[:, 1:2], rstat[:, 1:2], AF.Exp,
                                   scale=-0.5))
            mark(("finrstd", tt), s_ac)

        @blk.vector
        def _(v, tt=tt):
            waitm(v, ("finrstd", tt))
            v.tensor_scalar_mul(s_t[:], s_t[:], rstat[:, 1:2])
            v.tensor_mul(o_t[:], s_t[:], g_t[:])
            s_ve.inc(v.tensor_add(o_t[:], o_t[:], uo2_t[:]))
            mark(("fincv", tt), s_ve)

        @blk.sync
        def _(q, tt=tt):
            waitm(q, ("fincv", tt))
            s_st.inc(q.dma_start(out=out_d[tt * 128:(tt + 1) * 128, :],
                                 in_=o_t[:]), 16)
            mark(("fincomb", tt), s_st)

    if "drams" in dbg:
        @blk.sync
        def _(q):
            q.wait_ge(s_st.h, s_st.n)
            q.wait_ge(s_cc.h, s_cc.n)
            s_db.inc(q.dma_start(out=dbg_d["xn"][:], in_=xn_scr[:]), 16)
            s_db.inc(q.dma_start(out=dbg_d["ccin"][:], in_=cc_in[:]), 16)
            s_db.inc(q.dma_start(out=dbg_d["cc2in"][:], in_=cc2_in[:]), 16)
            s_db.inc(q.dma_start(out=dbg_d["gsum"][:], in_=gsum_d[:]), 16)
            s_db.inc(q.dma_start(out=dbg_d["gp0"][:], in_=gpart[0]), 16)
            s_db.inc(q.dma_start(out=dbg_d["ccout"][:], in_=cc_out[:]), 16)

    @blk.sync
    def _(q):
        q.wait_ge(s_st.h, s_st.n)
        if s_db.n:
            q.wait_ge(s_db.h, s_db.n)

    f3_es.close()
    es.close()
    nc.finalize()
    return nc


def kernel(**inputs):
    seq = int(np.asarray(inputs["L0_attn_u"]).shape[1])
    in_maps = host_prepare(inputs, seq)
    nc = build_program(seq=seq, dbg=DBG)
    res = run_bass_kernel_spmd(nc, in_maps, core_ids=list(range(NCORE)))
    out = np.concatenate([np.asarray(res.results[c]["out"])
                          for c in range(NCORE)], axis=0)
    if DBG:
        kernel.dbg = res.results
    return out.reshape(1, seq, D_MODEL)
